# version 21
# baseline (speedup 1.0000x reference)
import os
import numpy as np
import ml_dtypes
from contextlib import ExitStack

import concourse.bass as bass
import concourse.mybir as mybir
from concourse.bass_utils import run_bass_kernel_spmd

F32 = mybir.dt.float32
BF16 = mybir.dt.bfloat16
AF = mybir.ActivationFunctionType
ALU = mybir.AluOpType

ENGINES = ("sync", "tensor", "scalar", "vector", "gpsimd")

NCORES = 8
L = 8192
DM = 1024
NH = 16
HD = 64
TOWN = 2048
NT = 64
NTOWN = 16
DH = 4096
FC = DM // 128
HC = DH // 128
GRP = 8
NGRP = NT // GRP
HALF = 1024


def rw(t, n=128):
    return slice(t * n, (t + 1) * n)


class Sem:
    __slots__ = ("name", "n", "h")

    def __init__(self, name):
        self.name = name
        self.n = 0
        self.h = None


class Rec:

    def __init__(self, nc):
        self.nc = nc
        self.ops = {e: [] for e in ENGINES}

    def wait(self, eng, sem, val):
        if val <= 0:
            return
        self.ops[eng].append(("w", sem, val))

    def call(self, eng, fn, *a, **k):
        entry = ["c", fn, a, k, []]
        self.ops[eng].append(entry)
        return entry

    def inc(self, entry, sem, n=1):
        entry[4].append((sem, n))
        sem.n += n
        return sem.n

    def dma(self, eng, out, in_, sem, **k):
        e = self.call(eng, "dma_start", out, in_, **k)
        self.inc(e, sem, 16)
        return e

    def drain(self, eng):
        self.call(eng, "drain")

    def replay(self):
        import contextlib
        lp = (self.nc.allow_low_precision(reason="bf16 softmax recip broadcast")
              if getattr(self, "_allow_lp", False) else contextlib.nullcontext())
        with lp, self.nc.Block() as blk:
            for eng in ENGINES:
                if not self.ops[eng]:
                    continue
                ops = self.ops[eng]

                def make(ops):
                    def f(engobj):
                        for op in ops:
                            if op[0] == "w":
                                engobj.wait_ge(op[1].h, op[2])
                            else:
                                inst = getattr(engobj, op[1])(*op[2], **op[3])
                                for (s, n) in op[4]:
                                    inst = inst.then_inc(s.h, n)
                    return f

                getattr(blk, eng)(make(ops))


def build_nc(phases=6, debug=False, skip=(), p3_fill=3):
    nc = bass.Bass("TRN2", target_bir_lowering=False)

    scratch_kind = "ExternalOutput" if debug else "Internal"

    x_d = nc.dram_tensor("x", [L, DM], F32, kind="ExternalInput").ap()
    tabc_d = nc.dram_tensor("tabc", [L, 256], F32, kind="ExternalInput").ap()
    tabs_d = nc.dram_tensor("tabs", [L, 256], F32, kind="ExternalInput").ap()
    wqkv_d = nc.dram_tensor("wqkv", [DM, 3 * DM], F32, kind="ExternalInput").ap()
    wproj_d = nc.dram_tensor("wproj", [DM, DM], F32, kind="ExternalInput").ap()
    bproj_d = nc.dram_tensor("bproj", [1, DM], F32, kind="ExternalInput").ap()
    w1t_d = nc.dram_tensor("w1t", [DM, DH], F32, kind="ExternalInput").ap()
    w2t_d = nc.dram_tensor("w2t", [DM, DH], F32, kind="ExternalInput").ap()
    h1t_d = nc.dram_tensor("h1t", [DH, DH], F32, kind="ExternalInput").ap()
    h2t_d = nc.dram_tensor("h2t", [DH, DH], F32, kind="ExternalInput").ap()
    w3t_d = nc.dram_tensor("w3t", [DH, DM], F32, kind="ExternalInput").ap()
    scl_d = nc.dram_tensor("scl", [128, 4 * HC + FC], F32, kind="ExternalInput").ap()
    ident_d = nc.dram_tensor("ident", [128, 128], F32, kind="ExternalInput").ap()

    out_d = nc.dram_tensor("out", [TOWN, DM], F32, kind="ExternalOutput").ap()

    XL_d = nc.dram_tensor("XLs", [L, DM], BF16, kind=scratch_kind).ap()
    KR_d = nc.dram_tensor("KRs", [L, DM], BF16, kind=scratch_kind).ap()
    QR_d = nc.dram_tensor("QRs", [TOWN, DM], BF16, kind=scratch_kind).ap()
    VR_d = nc.dram_tensor("VRs", [NT, 128, NH * 65], BF16, kind=scratch_kind).ap()
    X2S_d = nc.dram_tensor("X2Ss", [TOWN, DM], F32, kind=scratch_kind).ap()
    X3_d = nc.dram_tensor("X3s", [TOWN, DM], BF16, kind=scratch_kind).ap()
    Y3_d = nc.dram_tensor("Y3s", [DM, TOWN], BF16, kind=scratch_kind).ap()
    SATdbg_d = (nc.dram_tensor("SATdbg", [128, FC * TOWN], BF16,
                               kind="ExternalOutput").ap() if debug else None)

    with ExitStack() as SP:
        alo = SP.enter_context
        ones_sb = alo(nc.sbuf_tensor("ones_sb", [1, 128], F32))
        ones_bf = alo(nc.sbuf_tensor("ones_bf", [1, 64], BF16))
        negb = alo(nc.sbuf_tensor("negb", [128, NH], F32))
        kqm = alo(nc.sbuf_tensor("kqm", [128, 32], F32))
        ident_sb = alo(nc.sbuf_tensor("ident_sb", [128, 128], F32))
        scl_sb = alo(nc.sbuf_tensor("scl_sb", [128, 4 * HC + FC], F32))
        s1_sb = scl_sb[:, 0 * HC:1 * HC]
        s2_sb = scl_sb[:, 1 * HC:2 * HC]
        sh1_sb = scl_sb[:, 2 * HC:3 * HC]
        sh2_sb = scl_sb[:, 3 * HC:4 * HC]
        s3_sb = scl_sb[:, 4 * HC:4 * HC + FC]
        m2_sb = alo(nc.sbuf_tensor("m2_sb", [32, 1], F32))
        bb_row = alo(nc.sbuf_tensor("bb_row", [1, 32], F32))
        bbp = alo(nc.sbuf_tensor("bbp", [1, NH], F32))

        psum_a = alo(nc.psum_tensor("psum_a", [128, 2048], F32))
        psum_b = alo(nc.psum_tensor("psum_b", [128, 2048], F32))

        sems = {}

        def S(name):
            s = Sem(name)
            s.h = alo(nc.semaphore(name))
            sems[name] = s
            return s

        s_x = S("s_x"); s_tab = S("s_tab"); s_lnv = S("s_lnv"); s_lns = S("s_lns")
        s_lnz = S("s_lnz"); s_xl = S("s_xl"); s_xlt = S("s_xlt"); s_wq = S("s_wq")
        s_mm = S("s_mm"); s_rope = S("s_rope"); s_vs = S("s_vs")
        s_kr = S("s_kr"); s_vr = S("s_vr"); s_qr = S("s_qr"); s_nrm = S("s_nrm")
        s_nsq = S("s_nsq"); s_nrd = S("s_nrd")
        s_bnd = S("s_bnd"); s_negb = S("s_negb"); s_cst = S("s_cst")
        s_kt = S("s_kt"); s_qt = S("s_qt"); s_vp = S("s_vp")
        s_smm = S("s_smm"); s_exp = S("s_exp"); s_pv = S("s_pv"); s_pvO = S("s_pvO")
        s_rec = S("s_rec"); s_bcm = S("s_bcm"); s_bce = S("s_bce"); s_att = S("s_att")
        s_oev = S("s_oev")
        s_xr = S("s_xr"); s_wp = S("s_wp"); s_pmm = S("s_pmm"); s_yev = S("s_yev")
        s_x2w = S("s_x2w"); s_x2s = S("s_x2s"); s_x3 = S("s_x3"); s_x3t = S("s_x3t")
        s_x3v = S("s_x3v")
        s_w12 = S("s_w12"); s_hwb = S("s_hwb"); s_fmm = S("s_fmm"); s_sil = S("s_sil")
        s_hv = S("s_hv"); s_gmm = S("s_gmm"); s_gelu = S("s_gelu")
        s_dmm = S("s_dmm"); s_y3e = S("s_y3e"); s_y3d = S("s_y3d"); s_y3t = S("s_y3t")
        s_x2l = S("s_x2l"); s_x3o = S("s_x3o"); s_out = S("s_out")

        with ExitStack() as SA:
            a2s = SA.enter_context
            sat = a2s(nc.sbuf_tensor("sat", [128, FC * TOWN], BF16))
            INNER = ExitStack()
            a2 = INNER.enter_context
            wqkv_sb = a2(nc.sbuf_tensor("wqkv_sb", [128, FC * 1536], BF16))
            NTAB = 6
            tabc_b = [a2(nc.sbuf_tensor(f"tabc{i}", [128, 256], F32)) for i in range(NTAB)]
            tabs_b = [a2(nc.sbuf_tensor(f"tabs{i}", [128, 256], F32)) for i in range(NTAB)]
            xlt = [a2(nc.sbuf_tensor(f"xlt{i}", [128, FC * GRP * 128], BF16)) for i in range(2)]
            kv_tile = [a2(nc.sbuf_tensor(f"kv{i}", [128, 512], BF16)) for i in range(2)]
            vstage = [a2(nc.sbuf_tensor(f"vs{i}", [128, 8 * 65], BF16)) for i in range(2)]
            q_tile = [a2(nc.sbuf_tensor(f"qt{i}", [128, 512], BF16)) for i in range(2)]
            t1 = a2(nc.sbuf_tensor("t1", [128, 256], F32))
            t2 = a2(nc.sbuf_tensor("t2", [128, 256], F32))
            nrm_t = a2(nc.sbuf_tensor("nrm_t", [128, 512], F32))
            nrm_c = a2(nc.sbuf_tensor("nrm_c", [128, 8], F32))
            R = Rec(nc)
            R._allow_lp = True
            R.call("gpsimd", "memset", ones_sb[:, :], 1.0)
            R.call("gpsimd", "memset", ones_bf[:, :], 1.0)
            for i in range(2):
                vv = vstage[i][:, :].rearrange("p (g x) -> p g x", x=65)[:, :, 64:65]
                R.call("gpsimd", "memset", vv, 1.0)
            R.call("gpsimd", "memset", kqm[:, :], 0.0)
            R.drain("gpsimd")
            R.dma("gpsimd", ident_sb[:, :], ident_d[:, :], s_cst)
            R.dma("gpsimd", scl_sb[:, :], scl_d[:, :], s_cst)

            with ExitStack() as TMP:
              atmp = TMP.enter_context
              x_in = [atmp(nc.sbuf_tensor(f"x_in{i}", [128, DM], F32)) for i in range(2)]
              ln_bf = [atmp(nc.sbuf_tensor(f"ln_bf{i}", [128, DM], BF16)) for i in range(2)]
              sstats = [atmp(nc.sbuf_tensor(f"st{i}", [128, 12], F32)) for i in range(2)]
              saggr = [atmp(nc.sbuf_tensor(f"sa{i}", [128, 2], F32)) for i in range(2)]
              svar = [atmp(nc.sbuf_tensor(f"sv{i}", [128, 1], F32)) for i in range(2)]
              srstd = [atmp(nc.sbuf_tensor(f"sr{i}", [128, 1], F32)) for i in range(2)]
              for t in range(NT):
                  sl = t % 2
                  if t >= 2:
                      R.wait("scalar", s_lnz, t - 1)
                  R.dma("scalar", x_in[sl][:, :], x_d[rw(t), :], s_x)
                  R.wait("vector", s_x, 16 * (t + 1))
                  R.call("vector", "bn_stats", sstats[sl][:, 0:6], x_in[sl][:, 0:512])
                  R.call("vector", "bn_stats", sstats[sl][:, 6:12], x_in[sl][:, 512:1024])
                  R.drain("vector")
                  R.call("vector", "bn_aggr", saggr[sl][:, :], sstats[sl][:, :])
                  R.drain("vector")
                  e = R.call("vector", "tensor_scalar", svar[sl][:, :], saggr[sl][:, 1:2],
                             1e-5, None, ALU.add)
                  R.inc(e, s_lnv)
                  R.wait("scalar", s_lnv, s_lnv.n)
                  e = R.call("scalar", "activation", srstd[sl][:, :], svar[sl][:, :], AF.Sqrt)
                  R.inc(e, s_lns)
                  R.wait("vector", s_lns, s_lns.n)
                  R.call("vector", "reciprocal", srstd[sl][:, :], srstd[sl][:, :])
                  R.drain("vector")
                  if t >= 2:
                      R.wait("vector", s_xl, 16 * (t - 1))
                  e = R.call("vector", "tensor_scalar", ln_bf[sl][:, :], x_in[sl][:, :],
                             saggr[sl][:, 0:1], srstd[sl][:, :], ALU.subtract, ALU.mult)
                  R.inc(e, s_lnz)
                  R.wait("sync", s_lnz, t + 1)
                  R.dma("sync", XL_d[rw(t), :], ln_bf[sl][:, :], s_xl)

            kT = [a2(nc.sbuf_tensor(f"kT{i}", [128, L], BF16)) for i in range(2)]
            qT = [a2(nc.sbuf_tensor(f"qT{i}", [128, TOWN], BF16)) for i in range(2)]
            vp = [a2(nc.sbuf_tensor(f"vp{i}", [128, 64 * 65], BF16)) for i in range(4)]
            sp = a2(nc.sbuf_tensor("sp", [128, 2 * 1024], BF16))
            srec = [a2(nc.sbuf_tensor(f"srec{i}", [1, 512], BF16)) for i in range(2)]
            oun = [a2(nc.sbuf_tensor(f"oun{i}", [65, 512], F32)) for i in range(4)]
            sbc = [a2(nc.sbuf_tensor(f"sbc{i}", [64, 512], F32)) for i in range(2)]

            pstate = {"xlt_use": {}, "pb_free": [None, None, None], "pb_rot": 0,
                      "wq_mm": 0, "kr_tot": {}, "vr_tot": {}, "qr_tot": {},
                      "rope_base": 0, "nrm_base": 0}
            PB = [psum_b[:, 512:1024], psum_b[:, 1024:1536], psum_b[:, 1536:2048]]

            def pb_next():
                sl = pstate["pb_rot"] % 3
                pstate["pb_rot"] += 1
                return sl, PB[sl]

            def rope_from_psum(pq3, dst3, c3, sn3):
                pr, pi = pq3[:, :, 0:32], pq3[:, :, 32:64]
                R.call("vector", "tensor_tensor", t1_3, pr, c3, ALU.mult)
                R.call("vector", "tensor_tensor", t2_3, pi, sn3, ALU.mult)
                R.drain("vector")
                R.call("vector", "tensor_tensor", dst3[:, :, 0:32], t1_3, t2_3,
                       ALU.subtract)
                R.call("vector", "tensor_tensor", t1_3, pr, sn3, ALU.mult)
                R.call("vector", "tensor_tensor", t2_3, pi, c3, ALU.mult)
                R.drain("vector")
                return R.call("vector", "tensor_tensor", dst3[:, :, 32:64], t1_3, t2_3,
                              ALU.add)

            t1_3 = t1[:, :].rearrange("p (h j) -> p h j", j=32)
            t2_3 = t2[:, :].rearrange("p (h j) -> p h j", j=32)

            def pass_gen(g):
                xlt_mm_cnt = {}
                rope_base = s_rope.n
                nrm_base = s_nrm.n
                kr_base = s_kr.n
                vr_base = s_vr.n
                qr_base = s_qr.n
                if g > 0:
                    R.wait("gpsimd", s_mm, pstate["wq_mm"])
                for bi, c0 in enumerate((g * 512, DM + g * 512, 2 * DM + g * 512)):
                    R.dma("gpsimd",
                          wqkv_sb[:, :].rearrange("p (c x) -> p c x", x=1536)[
                              :, :, bi * 512:(bi + 1) * 512],
                          wqkv_d.rearrange("(c p) o -> p c o", p=128)[
                              :, :, c0:c0 + 512], s_wq)
                yield
                for t in range(NT):
                    sl = t % 2
                    own = t < NTOWN
                    gr = t // GRP
                    if t % GRP == 0:
                        R.wait("sync", s_xl, 16 * min(NT, (gr + 1) * GRP))
                        bsel = gr % 2
                        if gr >= 2:
                            R.wait("sync", s_mm, xlt_mm_cnt[gr - 2])
                        elif bsel in pstate["xlt_use"]:
                            R.wait("sync", s_mm, pstate["xlt_use"][bsel])
                        for i in range(FC):
                            R.dma("sync", xlt[bsel][:, rw(i, GRP * 128)],
                                  XL_d[rw(gr, GRP * 128), rw(i)], s_xlt,
                                  transpose=True)
                        xlt_cnt = s_xlt.n
                        yield
                    if t >= NTAB:
                        R.wait("scalar", s_rope, rope_base + 2 * (t - NTAB) + 2)
                    R.dma("scalar", tabc_b[t % NTAB][:, :], tabc_d[rw(t), :], s_tab)
                    R.dma("scalar", tabs_b[t % NTAB][:, :], tabs_d[rw(t), :], s_tab)
                    toff = (t % GRP) * 128

                    def halfmm(bi, pX):
                        for i in range(FC):
                            e = R.call("tensor", "matmul", pX[:, 0:512],
                                       xlt[gr % 2][:, i * GRP * 128 + toff:
                                                   i * GRP * 128 + toff + 128],
                                       wqkv_sb[:, i * 1536 + bi * 512:
                                               i * 1536 + (bi + 1) * 512],
                                       start=(i == 0), stop=(i == FC - 1))
                        R.inc(e, s_mm)
                        return s_mm.n

                    R.wait("tensor", s_xlt, xlt_cnt)
                    R.wait("tensor", s_wq, 48 * (g + 1))
                    c3 = tabc_b[t % NTAB][:, :].rearrange("p (h j) -> p h j", j=32)
                    sn3 = tabs_b[t % NTAB][:, :].rearrange("p (h j) -> p h j", j=32)
                    slK, pK = pb_next()
                    if pstate["pb_free"][slK] is not None:
                        R.wait("tensor", *pstate["pb_free"][slK])
                    mmK = halfmm(1, pK)
                    R.wait("vector", s_mm, mmK)
                    R.wait("vector", s_tab, s_tab.n)
                    if t >= 2:
                        R.wait("vector", s_kr, kr_base + 16 * (t - 1))
                        R.wait("vector", s_nrm, nrm_base + t - 1)
                    e = rope_from_psum(
                        pK[:, 0:512].rearrange("p (h x) -> p h x", x=64),
                        kv_tile[sl][:, :].rearrange("p (h x) -> p h x", x=64),
                        c3, sn3)
                    R.inc(e, s_rope)
                    pstate["pb_free"][slK] = (s_rope, s_rope.n)
                    slV, pV = pb_next()
                    if pstate["pb_free"][slV] is not None:
                        R.wait("tensor", *pstate["pb_free"][slV])
                    mmV = halfmm(2, pV)
                    R.wait("scalar", s_mm, mmV)
                    if t >= 2:
                        R.wait("scalar", s_vr, vr_base + 16 * (t - 1))
                    sview = vstage[sl][:, :].rearrange("p (g2 x) -> p g2 x", x=65)[
                        :, :, 0:64]
                    pview = pV[:, 0:512].rearrange("p (g2 x) -> p g2 x", x=64)
                    e = R.call("scalar", "activation", sview, pview, AF.Copy)
                    R.inc(e, s_vs)
                    pstate["pb_free"][slV] = (s_vs, s_vs.n)
                    yield
                    if own:
                        slQ, pQ = pb_next()
                        if pstate["pb_free"][slQ] is not None:
                            R.wait("tensor", *pstate["pb_free"][slQ])
                        mmQ = halfmm(0, pQ)
                        R.wait("vector", s_mm, mmQ)
                        if t >= 2:
                            R.wait("vector", s_qr, qr_base + 16 * (t - 1))
                        e = rope_from_psum(
                            pQ[:, 0:512].rearrange("p (h x) -> p h x", x=64),
                            q_tile[sl][:, :].rearrange("p (h x) -> p h x", x=64),
                            c3, sn3)
                        R.inc(e, s_rope)
                        pstate["pb_free"][slQ] = (s_rope, s_rope.n)
                    else:
                        e = R.call("vector", "engine_nop")
                        R.inc(e, s_rope)
                    R.wait("sync", s_rope, rope_base + 2 * t + 2)
                    R.dma("sync", KR_d[rw(t), g * 512:(g + 1) * 512],
                          kv_tile[sl][:, :], s_kr)
                    if own:
                        R.dma("sync", QR_d[rw(t), g * 512:(g + 1) * 512],
                              q_tile[sl][:, :], s_qr)
                    R.wait("sync", s_vs, s_vs.n)
                    R.dma("sync", VR_d[t][:, g * 520:(g + 1) * 520],
                          vstage[sl][:, :], s_vr)
                    if t % GRP == GRP - 1:
                        xlt_mm_cnt[gr] = s_mm.n
                    R.wait("gpsimd", s_rope, rope_base + 2 * t + 2)
                    if s_nrd.n:
                        R.wait("gpsimd", s_nrd, s_nrd.n)
                    kv3 = kv_tile[sl][:, :].rearrange("p (h x) -> p h x", x=64)
                    nt3 = nrm_t[:, :].rearrange("p (h x) -> p h x", x=64)
                    e = R.call("gpsimd", "tensor_tensor", nt3, kv3, kv3, ALU.mult)
                    R.inc(e, s_nsq)
                    R.wait("vector", s_nsq, s_nsq.n)
                    R.call("vector", "tensor_reduce",
                           nrm_c[:, :].rearrange("p (h u) -> p h u", u=1), nt3,
                           mybir.AxisListType.X, ALU.add)
                    R.drain("vector")
                    R.call("vector", "tensor_tensor", kqm[:, g * 8:(g + 1) * 8],
                           kqm[:, g * 8:(g + 1) * 8], nrm_c[:, :], ALU.max)
                    R.drain("vector")
                    e = R.call("vector", "engine_nop")
                    R.inc(e, s_nrd)
                    if own:
                        q3 = q_tile[sl][:, :].rearrange("p (h x) -> p h x", x=64)
                        e = R.call("gpsimd", "tensor_tensor", nt3, q3, q3, ALU.mult)
                        R.inc(e, s_nsq)
                        R.wait("vector", s_nsq, s_nsq.n)
                        R.call("vector", "tensor_reduce",
                               nrm_c[:, :].rearrange("p (h u) -> p h u", u=1), nt3,
                               mybir.AxisListType.X, ALU.add)
                        R.drain("vector")
                        R.call("vector", "tensor_tensor",
                               kqm[:, 16 + g * 8:16 + (g + 1) * 8],
                               kqm[:, 16 + g * 8:16 + (g + 1) * 8], nrm_c[:, :],
                               ALU.max)
                        R.drain("vector")
                        e = R.call("vector", "engine_nop")
                        R.inc(e, s_nrd)
                    e = R.call("vector", "engine_nop")
                    R.inc(e, s_nrm)
                    yield
                pstate["wq_mm"] = s_mm.n
                pstate["xlt_use"][0] = s_mm.n
                pstate["xlt_use"][1] = s_mm.n
                pstate["kr_tot"][g] = s_kr.n
                pstate["vr_tot"][g] = s_vr.n
                pstate["qr_tot"][g] = s_qr.n
                sl0, pA = pb_next()
                if pstate["pb_free"][sl0] is not None:
                    R.wait("tensor", *pstate["pb_free"][sl0])
                sl1, pBb = pb_next()
                if pstate["pb_free"][sl1] is not None:
                    R.wait("tensor", *pstate["pb_free"][sl1])
                R.wait("tensor", s_nrm, s_nrm.n)
                R.wait("tensor", s_cst, 32)
                e = R.call("tensor", "transpose", pA[0:32, 0:128], kqm[:, :],
                           ident_sb[:, :])
                R.inc(e, s_bnd)
                R.wait("vector", s_bnd, s_bnd.n)
                e = R.call("vector", "reduce_max", m2_sb[:, :], pA[0:32, 0:128],
                           mybir.AxisListType.X)
                R.inc(e, s_bnd)
                R.wait("tensor", s_bnd, s_bnd.n)
                e = R.call("tensor", "transpose", pA[0:1, 0:32], m2_sb[:, :],
                           ident_sb[0:32, 0:32])
                R.inc(e, s_bnd)
                R.wait("vector", s_bnd, s_bnd.n)
                R.call("vector", "tensor_copy", bb_row[:, :], pA[0:1, 0:32])
                R.drain("vector")
                R.call("vector", "tensor_tensor", bbp[:, 0:8],
                       bb_row[:, g * 8:(g + 1) * 8],
                       bb_row[:, 16 + g * 8:16 + (g + 1) * 8], ALU.mult)
                R.drain("vector")
                R.call("vector", "reduce_max", bb_row[:, 0:1], bbp[:, 0:8],
                       mybir.AxisListType.X)
                R.drain("vector")
                e = R.call("vector", "engine_nop")
                R.inc(e, s_bnd)
                R.wait("scalar", s_bnd, s_bnd.n)
                e = R.call("scalar", "activation", bb_row[:, 0:1], bb_row[:, 0:1],
                           AF.Sqrt)
                R.inc(e, s_bnd)
                R.wait("vector", s_bnd, s_bnd.n)
                e = R.call("vector", "tensor_scalar", bb_row[:, 0:1], bb_row[:, 0:1],
                           -0.125, None, ALU.mult)
                R.inc(e, s_bnd)
                R.wait("tensor", s_bnd, s_bnd.n)
                e = R.call("tensor", "matmul", pBb[0:128, 0:1], ones_sb[:, :],
                           bb_row[:, 0:1], start=True, stop=True)
                R.inc(e, s_bnd)
                R.wait("scalar", s_bnd, s_bnd.n)
                e = R.call("scalar", "copy", negb[:, g:g + 1], pBb[0:128, 0:1])
                R.inc(e, s_negb)
                pstate["pb_free"][sl0] = (s_bnd, s_bnd.n - 1)
                pstate["pb_free"][sl1] = (s_negb, s_negb.n)

            genA = pass_gen(0)
            for _ in genA:
                pass
            passB = [pass_gen(1)]

            def pumpB(n=1):
                for _ in range(n):
                    if passB[0] is None:
                        return
                    try:
                        next(passB[0])
                    except StopIteration:
                        passB[0] = None

            Sv = psum_a[:, 0:1024]
            psO = [psum_a[0:65, 1024:1536], psum_a[0:65, 1536:2048]]
            psBC = psum_b[0:64, 0:512]

            glob_g = 0
            exp_at = {}
            pv_at = {}
            smm_at = {}
            last_smm_of_pair = [0] * 8
            last_pv_of_pair = [0] * 8
            oev_cnt_at = 0
            bcm_hist = []
            att_hist = []
            oun_free = {}
            pending_norm = []
            rec_hist = []

            def do_normalize(hp_, qb_):
                for e_ in range(2):
                    slot = (qb_ % 2) * 2 + e_
                    R.wait("tensor", s_rec, rec_hist.pop(0))
                    R.wait("tensor", s_bce, s_bce.n)
                    e = R.call("tensor", "matmul", psBC, ones_bf[:, :],
                               srec[e_][:, :], start=True, stop=True)
                    R.inc(e, s_bcm)
                    bcm_hist.append(s_bcm.n)
                    R.wait("scalar", s_bcm, s_bcm.n)
                    if len(att_hist) >= 2:
                        R.wait("scalar", s_att, att_hist[-2])
                    e = R.call("scalar", "copy", sbc[e_][:, :], psBC)
                    R.inc(e, s_bce)
                    R.wait("vector", s_bce, s_bce.n)
                    e = R.call("vector", "tensor_tensor",
                               sat[64 * e_:64 * e_ + 64,
                                   hp_ * TOWN + qb_ * 512:
                                   hp_ * TOWN + (qb_ + 1) * 512],
                               oun[slot][0:64, :], sbc[e_][:, :], ALU.mult)
                    R.inc(e, s_att)
                    att_hist.append(s_att.n)
                    oun_free[slot] = s_att.n

            for hp in range(8):
                g = hp // 4
                if g == 1 and passB[0] is not None:
                    while passB[0] is not None:
                        pumpB(1)
                R.wait("sync", s_kr, pstate["kr_tot"][g])
                R.wait("sync", s_qr, pstate["qr_tot"][g])
                if hp >= 2:
                    R.wait("sync", s_smm, last_smm_of_pair[hp - 2])
                R.dma("sync", kT[hp % 2][:, :], KR_d[:, rw(hp)], s_kt,
                      transpose=True)
                R.dma("sync", qT[hp % 2][:, :], QR_d[:, rw(hp)], s_qt,
                      transpose=True)
                for e_ in range(2):
                    h = 2 * hp + e_
                    R.wait("gpsimd", s_vr, pstate["vr_tot"][g])
                    if hp >= 2:
                        R.wait("gpsimd", s_pv, last_pv_of_pair[hp - 2])
                    R.dma("gpsimd",
                          vp[(hp % 2) * 2 + e_][:, :].rearrange(
                              "p (c f) -> p c f", f=65),
                          VR_d.rearrange("c p f -> p c f")[:, :, h * 65:(h + 1) * 65],
                          s_vp)
                R.wait("tensor", s_kt, 16 * (hp + 1))
                R.wait("tensor", s_qt, 16 * (hp + 1))
                R.wait("tensor", s_vp, 32 * (hp + 1))
                R.wait("scalar", s_negb, g + 1)
                for qb in range(4):

                    def emit_pv(gg, c):
                        R.wait("tensor", s_exp, exp_at[gg])
                        if c == 0:
                            R.wait("tensor", s_oev, oev_cnt_at)
                        sl2 = gg % 2
                        for e_ in range(2):
                            e = R.call("tensor", "matmul", psO[e_],
                                       vp[(hp % 2) * 2 + e_][:, c * 65:(c + 1) * 65],
                                       sp[:, sl2 * 1024 + e_ * 512:
                                          sl2 * 1024 + (e_ + 1) * 512],
                                       start=(c == 0), stop=(c == 63))
                            if c == 63:
                                R.inc(e, s_pvO)
                        if c == 63:
                            e2 = R.call("tensor", "nop")
                            R.inc(e2, s_pv)
                        else:
                            R.inc(e, s_pv)
                        pv_at[gg] = s_pv.n

                    for c in range(64):
                        gg = glob_g
                        sl2 = gg % 2
                        if gg >= 1:
                            R.wait("tensor", s_exp, exp_at[gg - 1])
                        for e_ in range(2):
                            e = R.call("tensor", "matmul",
                                       Sv[:, e_ * 512:(e_ + 1) * 512],
                                       kT[hp % 2][64 * e_:64 * e_ + 64, rw(c)],
                                       qT[hp % 2][64 * e_:64 * e_ + 64,
                                                  rw(qb, 512)],
                                       start=True, stop=True)
                        R.inc(e, s_smm)
                        smm_at[gg] = s_smm.n
                        R.wait("scalar", s_smm, smm_at[gg])
                        if gg >= 2:
                            R.wait("scalar", s_pv, pv_at[gg - 2])
                        e = R.call("scalar", "activation",
                                   sp[:, rw(sl2, 1024)], Sv[:, :],
                                   AF.Exp, bias=negb[:, g:g + 1], scale=0.125)
                        R.inc(e, s_exp)
                        exp_at[gg] = s_exp.n
                        if c >= 1:
                            emit_pv(gg - 1, c - 1)
                        if c == 16 and pending_norm:
                            do_normalize(*pending_norm.pop(0))
                        if c % 4 == 2:
                            pumpB(1)
                        glob_g += 1
                    emit_pv(glob_g - 1, 63)

                    for e_ in range(2):
                        slot = (qb % 2) * 2 + e_
                        R.wait("scalar", s_pvO, s_pvO.n - (1 - e_))
                        if slot in oun_free:
                            R.wait("scalar", s_att, oun_free[slot])
                        e = R.call("scalar", "copy", oun[slot][:, :], psO[e_])
                        R.inc(e, s_oev)
                    oev_cnt_at = s_oev.n
                    for e_ in range(2):
                        slot = (qb % 2) * 2 + e_
                        R.wait("vector", s_oev, oev_cnt_at - (1 - e_))
                        if len(bcm_hist) >= 2:
                            R.wait("vector", s_bcm, bcm_hist[-2 + e_])
                        e = R.call("vector", "reciprocal", srec[e_][:, :],
                                   oun[slot][64:65, :])
                        R.inc(e, s_rec)
                        rec_hist.append(s_rec.n)
                    pending_norm.append((hp, qb))
                last_pv_of_pair[hp] = s_pv.n
                last_smm_of_pair[hp] = s_smm.n
            while pending_norm:
                do_normalize(*pending_norm.pop(0))

            if debug:
                R.wait("sync", s_att, s_att.n)
                R.dma("sync", SATdbg_d, sat[:, :], s_kt)
                R.wait("sync", s_kt, s_kt.n)
            R.replay()
            p3_tot = {s.name: s.n for s in sems.values()}
            INNER.close()

            with ExitStack() as SB2:
                a4 = SB2.enter_context
                wproj_sb = a4(nc.sbuf_tensor("wproj_sb", [128, FC * DM], BF16))
                bpr_sb = a4(nc.sbuf_tensor("bpr_sb", [1, DM], F32))
                bias_bc = a4(nc.sbuf_tensor("bias_bc", [128, DM], F32))
                y_f = [a4(nc.sbuf_tensor(f"y_f{i}", [128, DM], F32)) for i in range(2)]
                x_res = [a4(nc.sbuf_tensor(f"x_res{i}", [128, DM], F32)) for i in range(2)]
                x2b = [a4(nc.sbuf_tensor(f"x2b{i}", [128, DM], F32)) for i in range(2)]
                x3b = [a4(nc.sbuf_tensor(f"x3b{i}", [128, DM], BF16)) for i in range(2)]
                t3f = a4(nc.sbuf_tensor("t3f", [128, DM], F32))
                st4 = [a4(nc.sbuf_tensor(f"st4{i}", [128, 12], F32)) for i in range(2)]
                ag4 = [a4(nc.sbuf_tensor(f"ag4{i}", [128, 2], F32)) for i in range(2)]
                vr4 = [a4(nc.sbuf_tensor(f"vr4{i}", [128, 1], F32)) for i in range(2)]
                rs4 = [a4(nc.sbuf_tensor(f"rs4{i}", [128, 1], F32)) for i in range(2)]

                R = Rec(nc)
                R.wait("gpsimd", s_smm, p3_tot["s_smm"])
                R.wait("gpsimd", s_pv, p3_tot["s_pv"])
                R.wait("gpsimd", s_att, p3_tot["s_att"])
                R.dma("gpsimd", wproj_sb[:, :].rearrange("p (c o) -> p c o", o=DM),
                      wproj_d.rearrange("(c p) o -> p c o", p=128), s_wp)
                R.dma("gpsimd", bpr_sb[:, :], bproj_d[:, :], s_cst)
                R.wait("tensor", s_att, p3_tot["s_att"])
                R.wait("tensor", s_oev, p3_tot["s_oev"])
                R.wait("tensor", s_wp, 16)
                R.wait("tensor", s_cst, s_cst.n)
                for j in range(2):
                    e = R.call("tensor", "matmul", psum_b[0:128, rw(j, 512)],
                               ones_sb[:, :], bpr_sb[:, rw(j, 512)],
                               start=True, stop=True)
                    R.inc(e, s_bnd)
                R.wait("scalar", s_bnd, s_bnd.n)
                e = R.call("scalar", "copy", bias_bc[:, :], psum_b[0:128, 0:1024])
                R.inc(e, s_negb)
                R.wait("vector", s_negb, s_negb.n)

                def ln_chain(sl2, src_ap, is_bf_src=False):
                    R.call("vector", "bn_stats", st4[sl2][:, 0:6], src_ap[:, 0:512])
                    R.call("vector", "bn_stats", st4[sl2][:, 6:12], src_ap[:, 512:1024])
                    R.drain("vector")
                    R.call("vector", "bn_aggr", ag4[sl2][:, :], st4[sl2][:, :])
                    R.drain("vector")
                    e = R.call("vector", "tensor_scalar", vr4[sl2][:, :],
                               ag4[sl2][:, 1:2], 1e-5, None, ALU.add)
                    R.inc(e, s_lnv)
                    R.wait("scalar", s_lnv, s_lnv.n)
                    e = R.call("scalar", "activation", rs4[sl2][:, :], vr4[sl2][:, :],
                               AF.Sqrt)
                    R.inc(e, s_lns)
                    R.wait("vector", s_lns, s_lns.n)
                    R.call("vector", "reciprocal", rs4[sl2][:, :], rs4[sl2][:, :])
                    R.drain("vector")

                for t in range(NTOWN):
                    sl2 = t % 2
                    if t >= 2:
                        R.wait("gpsimd", s_x2w, t - 1)
                    R.dma("gpsimd", x_res[sl2][:, :], x_d[rw(t), :], s_xr)
                    if t >= 2:
                        R.wait("tensor", s_yev, t - 1)
                    for ob in range(2):
                        for i in range(FC):
                            e = R.call("tensor", "matmul",
                                       psum_a[:, (t % 2) * 1024 + ob * 512:
                                              (t % 2) * 1024 + (ob + 1) * 512],
                                       sat[:, i * TOWN + t * 128:i * TOWN + (t + 1) * 128],
                                       wproj_sb[:, i * DM + ob * 512:
                                               i * DM + (ob + 1) * 512],
                                       start=(i == 0), stop=(i == FC - 1))
                    R.inc(e, s_pmm)
                    R.wait("vector", s_pmm, s_pmm.n)
                    py = psum_a[:, (t % 2) * 1024:(t % 2) * 1024 + 1024]
                    e = R.call("vector", "tensor_tensor", y_f[sl2][:, :], py,
                               bias_bc[:, :], ALU.add)
                    R.inc(e, s_yev)
                    R.drain("vector")
                    ln_chain(sl2, y_f[sl2])
                    R.call("vector", "tensor_scalar", t3f[:, :], y_f[sl2][:, :],
                           ag4[sl2][:, 0:1], rs4[sl2][:, :], ALU.subtract, ALU.mult)
                    R.drain("vector")
                    R.wait("vector", s_xr, 16 * (t + 1))
                    if t >= 2:
                        R.wait("vector", s_x2s, 16 * (t - 1))
                    e = R.call("vector", "tensor_tensor", x2b[sl2][:, :], t3f[:, :],
                               x_res[sl2][:, :], ALU.add)
                    R.inc(e, s_x2w)
                    R.drain("vector")
                    ln_chain(sl2, x2b[sl2])
                    if t >= 2:
                        R.wait("vector", s_x3, 16 * (t - 1))
                    e = R.call("vector", "tensor_scalar", x3b[sl2][:, :], x2b[sl2][:, :],
                               ag4[sl2][:, 0:1], rs4[sl2][:, :], ALU.subtract, ALU.mult)
                    R.inc(e, s_x3v)
                    R.wait("sync", s_x2w, t + 1)
                    R.dma("sync", X2S_d[rw(t), :], x2b[sl2][:, :], s_x2s)
                    R.wait("sync", s_x3v, t + 1)
                    R.dma("sync", X3_d[rw(t), :], x3b[sl2][:, :], s_x3)

                R.replay()

        if phases < 5:
            return nc
        p4_tot = {s.name: s.n for s in sems.values()}

        with ExitStack() as SC:
            a5 = SC.enter_context
            x3th = a5(nc.sbuf_tensor("x3th", [128, FC * HALF], BF16))
            hbuf = a5(nc.sbuf_tensor("hbuf", [128, HC * HALF], BF16))
            gbuf = a5(nc.sbuf_tensor("gbuf", [128, HC * HALF], BF16))
            w1b = [a5(nc.sbuf_tensor(f"w1b{i}", [128, FC * 128], BF16)) for i in range(2)]
            w2b = [a5(nc.sbuf_tensor(f"w2b{i}", [128, FC * 128], BF16)) for i in range(2)]
            hwb = [a5(nc.sbuf_tensor(f"hwb{i}", [128, HC * 128], BF16)) for i in range(2)]
            tsil = [a5(nc.sbuf_tensor(f"tsil{i}", [128, 512], F32)) for i in range(2)]
            y3s = [a5(nc.sbuf_tensor(f"y3s{i}", [128, 512], BF16)) for i in range(2)]
            y3tile = [a5(nc.sbuf_tensor(f"y3tile{i}", [128, DM], BF16)) for i in range(2)]
            x2t = [a5(nc.sbuf_tensor(f"x2t{i}", [128, DM], F32)) for i in range(2)]
            x3o = [a5(nc.sbuf_tensor(f"x3o{i}", [128, DM], F32)) for i in range(2)]
            t5f = a5(nc.sbuf_tensor("t5f", [128, DM], F32))
            st5 = a5(nc.sbuf_tensor("st5", [128, 12], F32))
            ag5 = a5(nc.sbuf_tensor("ag5", [128, 2], F32))
            vr5 = a5(nc.sbuf_tensor("vr5", [128, 1], F32))
            rs5 = a5(nc.sbuf_tensor("rs5", [128, 1], F32))

            R = Rec(nc)
            for eng in ("tensor", "scalar", "vector", "gpsimd"):
                R.wait(eng, s_pmm, p4_tot["s_pmm"])
                R.wait(eng, s_yev, p4_tot["s_yev"])

            def emit_stageE_tile(t):
                sl = t % 2
                R.wait("sync", s_y3d, 16 * 16 * (t // 8 + 1))
                if t >= 2:
                    R.wait("sync", s_x3o, t - 1)
                R.dma("sync", y3tile[sl][:, :], Y3_d[:, rw(t)], s_y3t, transpose=True)
                if t >= 2:
                    R.wait("gpsimd", s_x3o, t - 1)
                R.wait("gpsimd", s_x2s, 16 * (t + 1))
                R.dma("gpsimd", x2t[sl][:, :], X2S_d[rw(t), :], s_x2l)
                R.wait("vector", s_y3t, 16 * (t + 1))
                R.wait("vector", s_x2l, 16 * (t + 1))
                R.call("vector", "bn_stats", st5[:, 0:6], y3tile[sl][:, 0:512])
                R.call("vector", "bn_stats", st5[:, 6:12], y3tile[sl][:, 512:1024])
                R.drain("vector")
                R.call("vector", "bn_aggr", ag5[:, :], st5[:, :])
                R.drain("vector")
                e = R.call("vector", "tensor_scalar", vr5[:, :], ag5[:, 1:2],
                           1e-5, None, ALU.add)
                R.inc(e, s_lnv)
                R.wait("scalar", s_lnv, s_lnv.n)
                e = R.call("scalar", "activation", rs5[:, :], vr5[:, :], AF.Sqrt)
                R.inc(e, s_lns)
                R.wait("vector", s_lns, s_lns.n)
                R.call("vector", "reciprocal", rs5[:, :], rs5[:, :])
                R.drain("vector")
                R.call("vector", "tensor_scalar", t5f[:, :], y3tile[sl][:, :],
                       ag5[:, 0:1], rs5[:, :], ALU.subtract, ALU.mult)
                R.drain("vector")
                if t >= 2:
                    R.wait("vector", s_out, 16 * (t - 1))
                e = R.call("vector", "tensor_tensor", x3o[t % 2][:, :], t5f[:, :],
                           x2t[sl][:, :], ALU.add)
                R.inc(e, s_x3o)
                R.wait("sync", s_x3o, t + 1)
                R.dma("sync", out_d[rw(t), :], x3o[t % 2][:, :], s_out)

            pending_E = []

            fmm_at = {}
            silv = {}
            hv_at = {}
            w12_use = {}
            hw_use = {}
            y3s_last = {}
            for hf in range(2):
                tok0 = hf * HALF
                if hf == 1:
                    R.wait("tensor", s_y3e, s_y3e.n)
                    R.wait("vector", s_dmm, s_dmm.n)
                    R.wait("scalar", s_y3e, s_y3e.n)
                    R.wait("sync", s_fmm, s_fmm.n)
                R.wait("sync", s_x3, 16 * NTOWN // 2 * (hf + 1))
                for i in range(FC):
                    R.dma("sync", x3th[:, rw(i, HALF)],
                          X3_d[rw(hf, HALF), rw(i)], s_x3t, transpose=True)
                R.wait("tensor", s_x3t, s_x3t.n)
                for o in range(HC):
                    par = o % 2
                    if par in w12_use:
                        R.wait("gpsimd", s_fmm, w12_use[par])
                    R.dma("gpsimd", w1b[par][:, :].rearrange("p (c o2) -> p c o2", o2=128),
                          w1t_d.rearrange("(c p) o2 -> p c o2", p=128)[:, :, rw(o)], s_w12)
                    R.dma("gpsimd", w2b[par][:, :].rearrange("p (c o2) -> p c o2", o2=128),
                          w2t_d.rearrange("(c p) o2 -> p c o2", p=128)[:, :, rw(o)], s_w12)
                    R.wait("tensor", s_w12, s_w12.n)
                    if o % 4 == 3 and pending_E:
                        emit_stageE_tile(pending_E.pop(0))
                    pX = psum_a if par == 0 else psum_b
                    for tb in range(2):
                        key = (hf, o - 2, tb)
                        if o >= 2:
                            R.wait("tensor", s_sil, silv[key])
                            R.wait("tensor", s_hv, hv_at[key])
                        for which, (wb, pcol) in enumerate(
                                ((w1b[par], tb * 1024), (w2b[par], tb * 1024 + 512))):
                            for i in range(FC):
                                e = R.call("tensor", "matmul",
                                           pX[:, pcol:pcol + 512],
                                           wb[:, rw(i)],
                                           x3th[:, i * HALF + tb * 512:
                                                i * HALF + (tb + 1) * 512],
                                           start=(i == 0), stop=(i == FC - 1))
                            R.inc(e, s_fmm)
                            fmm_at[(hf, o, tb, "w1" if which == 0 else "w2")] = s_fmm.n
                        R.wait("scalar", s_fmm, fmm_at[(hf, o, tb, "w1")])
                        pk = (hf, o - 1, tb) if o >= 1 else (hf - 1, HC - 1, tb)
                        if pk in hv_at:
                            R.wait("scalar", s_hv, hv_at[pk])
                        e = R.call("scalar", "activation", tsil[tb][:, :],
                                   pX[:, tb * 1024:tb * 1024 + 512], AF.Silu,
                                   scale=s1_sb[:, o:o + 1])
                        R.inc(e, s_sil)
                        silv[(hf, o, tb)] = s_sil.n
                        R.wait("vector", s_sil, silv[(hf, o, tb)])
                        R.wait("vector", s_fmm, fmm_at[(hf, o, tb, "w2")])
                        e = R.call("vector", "scalar_tensor_tensor",
                                   hbuf[:, o * HALF + tb * 512:o * HALF + (tb + 1) * 512],
                                   pX[:, tb * 1024 + 512:tb * 1024 + 1024],
                                   s2_sb[:, o:o + 1], tsil[tb][:, :],
                                   ALU.mult, ALU.mult)
                        R.inc(e, s_hv)
                        hv_at[(hf, o, tb)] = s_hv.n
                    w12_use[par] = s_fmm.n
                hv_stageA = s_hv.n

                gelu_at = {}
                for o2 in range(HC):
                    par = o2 % 2
                    if par in hw_use:
                        R.wait("gpsimd", *hw_use[par])
                    R.dma("gpsimd", hwb[par][:, :].rearrange("p (c o3) -> p c o3", o3=128),
                          h1t_d.rearrange("(c p) o3 -> p c o3", p=128)[:, :, rw(o2)], s_hwb)
                    R.wait("tensor", s_hwb, s_hwb.n)
                    R.wait("tensor", s_hv, hv_stageA)
                    pX = psum_a if par == 0 else psum_b
                    for tb in range(2):
                        if o2 >= 2:
                            R.wait("tensor", s_gelu, gelu_at[("B", o2 - 2, tb)])
                        for i in range(HC):
                            e = R.call("tensor", "matmul", pX[:, rw(tb, 512)],
                                       hwb[par][:, rw(i)],
                                       hbuf[:, i * HALF + tb * 512:
                                            i * HALF + (tb + 1) * 512],
                                       start=(i == 0), stop=(i == HC - 1))
                        R.inc(e, s_gmm)
                        R.wait("scalar", s_gmm, s_gmm.n)
                        e = R.call("scalar", "activation",
                                   gbuf[:, o2 * HALF + tb * 512:o2 * HALF + (tb + 1) * 512],
                                   pX[:, rw(tb, 512)], AF.Gelu, scale=sh1_sb[:, o2:o2 + 1])
                        R.inc(e, s_gelu)
                        gelu_at[("B", o2, tb)] = s_gelu.n
                    hw_use[par] = (s_gmm, s_gmm.n)
                gelu_stageB = s_gelu.n

                for o2 in range(HC):
                    par = o2 % 2
                    if par in hw_use:
                        R.wait("gpsimd", *hw_use[par])
                    R.dma("gpsimd", hwb[par][:, :].rearrange("p (c o3) -> p c o3", o3=128),
                          h2t_d.rearrange("(c p) o3 -> p c o3", p=128)[:, :, rw(o2)], s_hwb)
                    R.wait("tensor", s_hwb, s_hwb.n)
                    R.wait("tensor", s_gelu, gelu_stageB)
                    pX = psum_a if par == 0 else psum_b
                    for tb in range(2):
                        if o2 >= 2:
                            R.wait("tensor", s_gelu, gelu_at[("C", o2 - 2, tb)])
                        for i in range(HC):
                            e = R.call("tensor", "matmul", pX[:, rw(tb, 512)],
                                       hwb[par][:, rw(i)],
                                       gbuf[:, i * HALF + tb * 512:
                                            i * HALF + (tb + 1) * 512],
                                       start=(i == 0), stop=(i == HC - 1))
                        R.inc(e, s_gmm)
                        R.wait("scalar", s_gmm, s_gmm.n)
                        e = R.call("scalar", "activation",
                                   hbuf[:, o2 * HALF + tb * 512:o2 * HALF + (tb + 1) * 512],
                                   pX[:, rw(tb, 512)], AF.Gelu, scale=sh2_sb[:, o2:o2 + 1])
                        R.inc(e, s_gelu)
                        gelu_at[("C", o2, tb)] = s_gelu.n
                    hw_use[par] = (s_gmm, s_gmm.n)
                gelu_stageC = s_gelu.n

                y3e_at = {}
                y3d_at = {}
                for o3 in range(FC):
                    par = o3 % 2
                    if par in hw_use:
                        R.wait("gpsimd", *hw_use[par])
                    R.dma("gpsimd", hwb[par][:, :].rearrange("p (c o4) -> p c o4", o4=128),
                          w3t_d.rearrange("(c p) o4 -> p c o4", p=128)[:, :, rw(o3)], s_hwb)
                    R.wait("tensor", s_hwb, s_hwb.n)
                    R.wait("tensor", s_gelu, gelu_stageC)
                    pX = psum_a if par == 0 else psum_b
                    for tb in range(2):
                        if o3 >= 2:
                            R.wait("tensor", s_y3e, y3e_at[(o3 - 2, tb)])
                        for i in range(HC):
                            e = R.call("tensor", "matmul", pX[:, rw(tb, 512)],
                                       hwb[par][:, rw(i)],
                                       hbuf[:, i * HALF + tb * 512:
                                            i * HALF + (tb + 1) * 512],
                                       start=(i == 0), stop=(i == HC - 1))
                        R.inc(e, s_dmm)
                        R.wait("vector", s_dmm, s_dmm.n)
                        if tb in y3s_last:
                            R.wait("vector", s_y3d, y3s_last[tb])
                        e = R.call("vector", "tensor_scalar", y3s[tb][:, :],
                                   pX[:, rw(tb, 512)], s3_sb[:, o3:o3 + 1], None,
                                   ALU.mult)
                        R.inc(e, s_y3e)
                        y3e_at[(o3, tb)] = s_y3e.n
                        R.wait("sync", s_y3e, s_y3e.n)
                        R.dma("sync", Y3_d[rw(o3), tok0 + tb * 512:tok0 + (tb + 1) * 512],
                              y3s[tb][:, :], s_y3d)
                        y3d_at[(o3, tb)] = s_y3d.n
                        y3s_last[tb] = s_y3d.n
                    hw_use[par] = (s_dmm, s_dmm.n)
                pending_E.extend(range(hf * 8, hf * 8 + 8))

            while pending_E:
                emit_stageE_tile(pending_E.pop(0))

            R.wait("sync", s_out, 16 * NTOWN)
            R.replay()

    return nc



def _prep_inputs(inputs):
    x = np.asarray(inputs["L0_attn_x"], dtype=np.float32)
    wqkv = np.asarray(inputs["L0_attn_w_qkv"], dtype=np.float32)
    wproj = np.asarray(inputs["L0_attn_w_proj"], dtype=np.float32)
    bproj = np.asarray(inputs["L0_attn_b_proj"], dtype=np.float32)
    w1 = np.asarray(inputs["L0_ffn_w1_w"], dtype=np.float32)
    w2 = np.asarray(inputs["L0_ffn_w2_w"], dtype=np.float32)
    h1 = np.asarray(inputs["L0_ffn_h1_w"], dtype=np.float32)
    h2 = np.asarray(inputs["L0_ffn_h2_w"], dtype=np.float32)
    w3 = np.asarray(inputs["L0_ffn_w3_w"], dtype=np.float32)

    idx = []
    for blk in range(2):
        for h in range(NH):
            base = blk * DM + h * HD
            idx += [base + 2 * j for j in range(32)]
            idx += [base + 2 * j + 1 for j in range(32)]
    idx += list(range(2 * DM, 3 * DM))
    wqkv_p = np.ascontiguousarray(wqkv[:, idx])

    a = np.arange(0, HD, 2, dtype=np.float32) / np.float32(HD)
    theta = (np.float32(1.0) / (np.float32(10000.0) ** a)) * np.float32(3.0)
    ang = np.arange(L, dtype=np.float32)[:, None] * theta[None, :]
    cos = np.cos(ang).astype(np.float32)
    sin = np.sin(ang).astype(np.float32)
    cos_rep = np.tile(cos, (1, 8))
    sin_rep = np.tile(sin, (1, 8))

    shared = {
        "wqkv": wqkv_p,
        "wproj": np.ascontiguousarray(wproj),
        "bproj": bproj.reshape(1, DM),
        "w1t": np.ascontiguousarray(w1.T),
        "w2t": np.ascontiguousarray(w2.T),
        "h1t": np.ascontiguousarray(h1.T),
        "h2t": np.ascontiguousarray(h2.T),
        "w3t": np.ascontiguousarray(w3.T),
        "scl": np.ascontiguousarray(np.concatenate(
            [np.asarray(inputs["L0_ffn_w1_s"], np.float32).reshape(HC, 128).T,
             np.asarray(inputs["L0_ffn_w2_s"], np.float32).reshape(HC, 128).T,
             np.asarray(inputs["L0_ffn_h1_s"], np.float32).reshape(HC, 128).T,
             np.asarray(inputs["L0_ffn_h2_s"], np.float32).reshape(HC, 128).T,
             np.asarray(inputs["L0_ffn_w3_s"], np.float32).reshape(FC, 128).T],
            axis=1)),
        "ident": np.eye(128, dtype=np.float32),
    }
    in_maps = []
    for c in range(NCORES):
        b, s = c // 4, c % 4
        m = dict(shared)
        m["x"] = np.ascontiguousarray(np.roll(x[b], -s * TOWN, axis=0))
        m["tabc"] = np.ascontiguousarray(np.roll(cos_rep, -s * TOWN, axis=0))
        m["tabs"] = np.ascontiguousarray(np.roll(sin_rep, -s * TOWN, axis=0))
        in_maps.append(m)
    return in_maps


def kernel(**inputs):
    in_maps = _prep_inputs(inputs)
    nc = build_nc()
    res = run_bass_kernel_spmd(nc, in_maps, core_ids=list(range(NCORES)))
    out = np.concatenate([np.asarray(r["out"], dtype=np.float32)
                          for r in res.results], axis=0)
    return out.reshape(2, L, DM)


# version 22 (final)
# speedup vs baseline: 1.5329x; 1.5329x over previous
import os
import numpy as np
import ml_dtypes
from contextlib import ExitStack

import concourse.bass as bass
import concourse.mybir as mybir
from concourse.bass_utils import run_bass_kernel_spmd

F32 = mybir.dt.float32
BF16 = mybir.dt.bfloat16
AF = mybir.ActivationFunctionType
ALU = mybir.AluOpType

ENGINES = ("sync", "tensor", "scalar", "vector", "gpsimd")

NCORES = 8
L = 8192
DM = 1024
NH = 16
HD = 64
TOWN = 2048
NT = 64
NTOWN = 16
DH = 4096
FC = DM // 128
HC = DH // 128
GRP = 8
NGRP = NT // GRP
HALF = 1024


def rw(t, n=128):
    return slice(t * n, (t + 1) * n)


class Sem:
    __slots__ = ("name", "n", "h")

    def __init__(self, name):
        self.name = name
        self.n = 0
        self.h = None


class Rec:

    def __init__(self, nc):
        self.nc = nc
        self.ops = {e: [] for e in ENGINES}

    def wait(self, eng, sem, val):
        if val <= 0:
            return
        self.ops[eng].append(("w", sem, val))

    def call(self, eng, fn, *a, **k):
        entry = ["c", fn, a, k, []]
        self.ops[eng].append(entry)
        return entry

    def inc(self, entry, sem, n=1):
        entry[4].append((sem, n))
        sem.n += n
        return sem.n

    def dma(self, eng, out, in_, sem, **k):
        e = self.call(eng, "dma_start", out, in_, **k)
        self.inc(e, sem, 16)
        return e

    def drain(self, eng):
        self.call(eng, "drain")

    def replay(self):
        import contextlib
        lp = (self.nc.allow_low_precision(reason="bf16 softmax recip broadcast")
              if getattr(self, "_allow_lp", False) else contextlib.nullcontext())
        with lp, self.nc.Block() as blk:
            for eng in ENGINES:
                if not self.ops[eng]:
                    continue
                ops = self.ops[eng]

                def make(ops):
                    def f(engobj):
                        for op in ops:
                            if op[0] == "w":
                                engobj.wait_ge(op[1].h, op[2])
                            else:
                                inst = getattr(engobj, op[1])(*op[2], **op[3])
                                for (s, n) in op[4]:
                                    inst = inst.then_inc(s.h, n)
                    return f

                getattr(blk, eng)(make(ops))


def build_nc(phases=6, debug=False, skip=(), p3_fill=3):
    nc = bass.Bass("TRN2", target_bir_lowering=False)

    scratch_kind = "ExternalOutput" if debug else "Internal"

    x_d = nc.dram_tensor("x", [L, DM], F32, kind="ExternalInput").ap()
    tabc_d = nc.dram_tensor("tabc", [L, 256], F32, kind="ExternalInput").ap()
    tabs_d = nc.dram_tensor("tabs", [L, 256], F32, kind="ExternalInput").ap()
    wqkv_d = nc.dram_tensor("wqkv", [DM, 3 * DM], F32, kind="ExternalInput").ap()
    wproj_d = nc.dram_tensor("wproj", [DM, DM], F32, kind="ExternalInput").ap()
    bproj_d = nc.dram_tensor("bproj", [1, DM], F32, kind="ExternalInput").ap()
    w1t_d = nc.dram_tensor("w1t", [DM, DH], F32, kind="ExternalInput").ap()
    w2t_d = nc.dram_tensor("w2t", [DM, DH], F32, kind="ExternalInput").ap()
    h1t_d = nc.dram_tensor("h1t", [DH, DH], F32, kind="ExternalInput").ap()
    h2t_d = nc.dram_tensor("h2t", [DH, DH], F32, kind="ExternalInput").ap()
    w3t_d = nc.dram_tensor("w3t", [DH, DM], F32, kind="ExternalInput").ap()
    scl_d = nc.dram_tensor("scl", [128, 4 * HC + FC], F32, kind="ExternalInput").ap()
    ident_d = nc.dram_tensor("ident", [128, 128], F32, kind="ExternalInput").ap()

    out_d = nc.dram_tensor("out", [TOWN, DM], F32, kind="ExternalOutput").ap()

    XL_d = nc.dram_tensor("XLs", [L, DM], BF16, kind=scratch_kind).ap()
    KR_d = nc.dram_tensor("KRs", [L, DM], BF16, kind=scratch_kind).ap()
    QR_d = nc.dram_tensor("QRs", [TOWN, DM], BF16, kind=scratch_kind).ap()
    VR_d = nc.dram_tensor("VRs", [NT, 128, NH * 65], BF16, kind=scratch_kind).ap()
    X2S_d = nc.dram_tensor("X2Ss", [TOWN, DM], F32, kind=scratch_kind).ap()
    X3_d = nc.dram_tensor("X3s", [TOWN, DM], BF16, kind=scratch_kind).ap()
    Y3_d = nc.dram_tensor("Y3s", [DM, TOWN], BF16, kind=scratch_kind).ap()
    SATdbg_d = (nc.dram_tensor("SATdbg", [128, FC * TOWN], BF16,
                               kind="ExternalOutput").ap() if debug else None)

    with ExitStack() as SP:
        alo = SP.enter_context
        ones_sb = alo(nc.sbuf_tensor("ones_sb", [1, 128], F32))
        ones_bf = alo(nc.sbuf_tensor("ones_bf", [1, 64], BF16))
        negb = alo(nc.sbuf_tensor("negb", [128, NH], F32))
        kqm = alo(nc.sbuf_tensor("kqm", [128, 32], F32))
        ident_sb = alo(nc.sbuf_tensor("ident_sb", [128, 128], F32))
        scl_sb = alo(nc.sbuf_tensor("scl_sb", [128, 4 * HC + FC], F32))
        s1_sb = scl_sb[:, 0 * HC:1 * HC]
        s2_sb = scl_sb[:, 1 * HC:2 * HC]
        sh1_sb = scl_sb[:, 2 * HC:3 * HC]
        sh2_sb = scl_sb[:, 3 * HC:4 * HC]
        s3_sb = scl_sb[:, 4 * HC:4 * HC + FC]
        m2_sb = alo(nc.sbuf_tensor("m2_sb", [32, 1], F32))
        bb_row = alo(nc.sbuf_tensor("bb_row", [1, 32], F32))
        bbp = alo(nc.sbuf_tensor("bbp", [1, NH], F32))

        psum_a = alo(nc.psum_tensor("psum_a", [128, 2048], F32))
        psum_b = alo(nc.psum_tensor("psum_b", [128, 2048], F32))

        sems = {}

        def S(name):
            s = Sem(name)
            s.h = alo(nc.semaphore(name))
            sems[name] = s
            return s

        s_x = S("s_x"); s_tab = S("s_tab"); s_lnv = S("s_lnv"); s_lns = S("s_lns")
        s_lnz = S("s_lnz"); s_xl = S("s_xl"); s_xlt = S("s_xlt"); s_wq = S("s_wq")
        s_mm = S("s_mm"); s_rope = S("s_rope"); s_vs = S("s_vs")
        s_kr = S("s_kr"); s_vr = S("s_vr"); s_qr = S("s_qr"); s_nrm = S("s_nrm")
        s_nsq = S("s_nsq"); s_nrd = S("s_nrd")
        s_bnd = S("s_bnd"); s_negb = S("s_negb"); s_cst = S("s_cst")
        s_kt = S("s_kt"); s_qt = S("s_qt"); s_vp = S("s_vp")
        s_smm = S("s_smm"); s_exp = S("s_exp"); s_pv = S("s_pv"); s_pvO = S("s_pvO")
        s_rec = S("s_rec"); s_bcm = S("s_bcm"); s_bce = S("s_bce"); s_att = S("s_att")
        s_oev = S("s_oev")
        s_xr = S("s_xr"); s_wp = S("s_wp"); s_pmm = S("s_pmm"); s_yev = S("s_yev")
        s_x2w = S("s_x2w"); s_x2s = S("s_x2s"); s_x3 = S("s_x3"); s_x3t = S("s_x3t")
        s_x3v = S("s_x3v")
        s_w12 = S("s_w12"); s_hwb = S("s_hwb"); s_fmm = S("s_fmm"); s_sil = S("s_sil")
        s_hv = S("s_hv"); s_gmm = S("s_gmm"); s_gelu = S("s_gelu")
        s_dmm = S("s_dmm"); s_y3e = S("s_y3e"); s_y3d = S("s_y3d"); s_y3t = S("s_y3t")
        s_x2l = S("s_x2l"); s_x3o = S("s_x3o"); s_out = S("s_out")

        with ExitStack() as SA:
            a2 = SA.enter_context
            wqkv_sb = a2(nc.sbuf_tensor("wqkv_sb", [128, FC * 3 * DM], BF16))
            x_in = [a2(nc.sbuf_tensor(f"x_in{i}", [128, DM], F32)) for i in range(2)]
            NTAB = 12
            tabc_b = [a2(nc.sbuf_tensor(f"tabc{i}", [128, 256], F32)) for i in range(NTAB)]
            tabs_b = [a2(nc.sbuf_tensor(f"tabs{i}", [128, 256], F32)) for i in range(NTAB)]
            xlt = [a2(nc.sbuf_tensor(f"xlt{i}", [128, FC * GRP * 128], BF16)) for i in range(2)]
            ln_bf = [a2(nc.sbuf_tensor(f"ln_bf{i}", [128, DM], BF16)) for i in range(2)]
            kv_tile = [a2(nc.sbuf_tensor(f"kv{i}", [128, DM], BF16)) for i in range(2)]
            vstage = [a2(nc.sbuf_tensor(f"vs{i}", [128, NH * 65], BF16)) for i in range(2)]
            q_tile = [a2(nc.sbuf_tensor(f"qt{i}", [128, DM], BF16)) for i in range(2)]
            t1 = a2(nc.sbuf_tensor("t1", [128, 256], F32))
            t2 = a2(nc.sbuf_tensor("t2", [128, 256], F32))
            t1q = a2(nc.sbuf_tensor("t1q", [128, 256], F32))
            t2q = a2(nc.sbuf_tensor("t2q", [128, 256], F32))
            sstats = [a2(nc.sbuf_tensor(f"st{i}", [128, 12], F32)) for i in range(2)]
            saggr = [a2(nc.sbuf_tensor(f"sa{i}", [128, 2], F32)) for i in range(2)]
            svar = [a2(nc.sbuf_tensor(f"sv{i}", [128, 1], F32)) for i in range(2)]
            srstd = [a2(nc.sbuf_tensor(f"sr{i}", [128, 1], F32)) for i in range(2)]
            nrm_t = a2(nc.sbuf_tensor("nrm_t", [128, DM], F32))
            nrm_c = a2(nc.sbuf_tensor("nrm_c", [128, NH], F32))

            R = Rec(nc)
            e = R.call("gpsimd", "memset", ones_sb[:, :], 1.0)
            e = R.call("gpsimd", "memset", ones_bf[:, :], 1.0)
            for i in range(2):
                vv = vstage[i][:, :].rearrange("p (g x) -> p g x", x=65)[:, :, 64:65]
                R.call("gpsimd", "memset", vv, 1.0)
            R.call("gpsimd", "memset", kqm[:, :], 0.0)
            R.drain("gpsimd")
            R.dma("gpsimd", wqkv_sb[:, :].rearrange("p (c o) -> p c o", o=3 * DM),
                  wqkv_d.rearrange("(c p) o -> p c o", p=128), s_wq)
            R.dma("gpsimd", ident_sb[:, :], ident_d[:, :], s_cst)
            R.dma("gpsimd", scl_sb[:, :], scl_d[:, :], s_cst)

            cntK = [0] * NT
            cntV = [0] * NT
            cntLast = [0] * NT
            cntQ = [0] * NT
            ropecnt = [0] * NT

            def norms_job(tt, ropec, own_t):
                slj = tt % 2
                srcs = [(kv_tile[slj], 0)] + ([(q_tile[slj], NH)] if own_t else [])
                for (tile_, coff) in srcs:
                    R.wait("gpsimd", s_rope, ropec)
                    if nrm_state["mx"]:
                        R.wait("gpsimd", s_nrm, nrm_state["mx"])
                    v3 = tile_[:, :].rearrange("p (h x) -> p h x", x=64)
                    nt3 = nrm_t[:, :].rearrange("p (h x) -> p h x", x=64)
                    e = R.call("gpsimd", "tensor_tensor", nt3, v3, v3, ALU.mult)
                    R.inc(e, s_nsq)
                    R.wait("vector", s_nsq, s_nsq.n)
                    R.call("vector", "tensor_reduce",
                           nrm_c[:, :].rearrange("p (h u) -> p h u", u=1), nt3,
                           mybir.AxisListType.X, ALU.add)
                    R.drain("vector")
                    e = R.call("vector", "tensor_tensor", kqm[:, coff:coff + NH],
                               kqm[:, coff:coff + NH], nrm_c[:, :], ALU.max)
                    R.drain("vector")
                    e = R.call("vector", "engine_nop")
                    R.inc(e, s_nrm)
                    nrm_state["mx"] = s_nrm.n

            nrm_state = {"mx": 0}
            norm_q = []

            def emit_loads_ln(t):
                sl = t % 2
                gr = t // GRP
                if t >= 2:
                    R.wait("scalar", s_lnz, t - 1)
                if t >= NTAB:
                    R.wait("scalar", s_rope, t - NTAB + 1)
                R.dma("scalar", x_in[sl][:, :], x_d[rw(t), :], s_x)
                R.dma("scalar", tabc_b[t % NTAB][:, :], tabc_d[rw(t), :], s_tab)
                R.dma("scalar", tabs_b[t % NTAB][:, :], tabs_d[rw(t), :], s_tab)
                R.wait("vector", s_x, 16 * (t + 1))
                R.call("vector", "bn_stats", sstats[sl][:, 0:6], x_in[sl][:, 0:512])
                R.call("vector", "bn_stats", sstats[sl][:, 6:12], x_in[sl][:, 512:1024])
                R.drain("vector")
                R.call("vector", "bn_aggr", saggr[sl][:, :], sstats[sl][:, :])
                R.drain("vector")
                e = R.call("vector", "tensor_scalar", svar[sl][:, :], saggr[sl][:, 1:2],
                           1e-5, None, ALU.add)
                R.inc(e, s_lnv)
                R.wait("scalar", s_lnv, s_lnv.n)
                e = R.call("scalar", "activation", srstd[sl][:, :], svar[sl][:, :], AF.Sqrt)
                R.inc(e, s_lns)
                R.wait("vector", s_lns, s_lns.n)
                R.call("vector", "reciprocal", srstd[sl][:, :], srstd[sl][:, :])
                R.drain("vector")
                if t >= 2:
                    R.wait("vector", s_xl, 16 * (t - 1))
                e = R.call("vector", "tensor_scalar", ln_bf[sl][:, :], x_in[sl][:, :],
                           saggr[sl][:, 0:1], srstd[sl][:, :], ALU.subtract, ALU.mult)
                R.inc(e, s_lnz)
                R.wait("sync", s_lnz, t + 1)
                R.dma("sync", XL_d[rw(t), :], ln_bf[sl][:, :], s_xl)
                if t % GRP == GRP - 1:
                    R.wait("sync", s_xl, 16 * (t + 1))
                    if gr >= 2:
                        R.wait("sync", s_mm, cntLast[(gr - 2) * GRP + GRP - 1])
                    for i in range(FC):
                        R.dma("sync", xlt[gr % 2][:, rw(i, GRP * 128)],
                              XL_d[rw(gr, GRP * 128), rw(i)], s_xlt, transpose=True)

            def emit_qkv_mm(t):
                gr = t // GRP
                own = t < NTOWN
                R.wait("tensor", s_xlt, 16 * FC * (gr + 1))
                if t == 0:
                    R.wait("tensor", s_wq, 16)
                if t >= 1:
                    R.wait("tensor", s_rope, t)
                    R.wait("tensor", s_vs, t)
                toff = (t % GRP) * 128
                for ob in range(4):
                    for i in range(FC):
                        e = R.call("tensor", "matmul", psum_a[:, rw(ob, 512)],
                                   xlt[gr % 2][:, i * GRP * 128 + toff:
                                               i * GRP * 128 + toff + 128],
                                   wqkv_sb[:, i * 3 * DM + DM + ob * 512:
                                           i * 3 * DM + DM + (ob + 1) * 512],
                                   start=(i == 0), stop=(i == FC - 1))
                    R.inc(e, s_mm)
                    if ob == 1:
                        cntK[t] = s_mm.n
                    if ob == 3:
                        cntV[t] = s_mm.n
                if own:
                    for ob in range(2):
                        for i in range(FC):
                            e = R.call("tensor", "matmul", psum_b[:, rw(ob, 512)],
                                       xlt[gr % 2][:, i * GRP * 128 + toff:
                                                   i * GRP * 128 + toff + 128],
                                       wqkv_sb[:, i * 3 * DM + ob * 512:
                                               i * 3 * DM + (ob + 1) * 512],
                                       start=(i == 0), stop=(i == FC - 1))
                        R.inc(e, s_mm)
                    cntQ[t] = s_mm.n
                cntLast[t] = s_mm.n

            def rope_block(pq, dst, c3, sn3, ta, tb_):
                pr, pi = pq[:, :, 0:32], pq[:, :, 32:64]
                R.call("vector", "tensor_tensor", ta, pr, c3, ALU.mult)
                R.call("vector", "tensor_tensor", tb_, pi, sn3, ALU.mult)
                R.drain("vector")
                R.call("vector", "tensor_tensor", dst[:, :, 0:32], ta, tb_, ALU.subtract)
                R.call("vector", "tensor_tensor", ta, pr, sn3, ALU.mult)
                R.call("vector", "tensor_tensor", tb_, pi, c3, ALU.mult)
                R.drain("vector")
                return R.call("vector", "tensor_tensor", dst[:, :, 32:64], ta, tb_, ALU.add)

            def emit_rope(t):
                sl = t % 2
                own = t < NTOWN
                c3 = tabc_b[t % NTAB][:, :].rearrange("p (h j) -> p h j", j=32)
                sn3 = tabs_b[t % NTAB][:, :].rearrange("p (h j) -> p h j", j=32)
                t13 = t1[:, :].rearrange("p (h j) -> p h j", j=32)
                t23 = t2[:, :].rearrange("p (h j) -> p h j", j=32)
                R.wait("vector", s_mm, cntK[t])
                R.wait("vector", s_tab, 32 * (t + 1))
                if t >= 2:
                    R.wait("vector", s_kr, 16 * (t - 1))
                    R.wait("vector", s_nrm, (t - 1) + min(t - 1, NTOWN))
                for ob in range(2):
                    pq = psum_a[:, rw(ob, 512)].rearrange("p (h x) -> p h x", x=64)
                    kk = kv_tile[sl][:, rw(ob, 512)].rearrange("p (h x) -> p h x", x=64)
                    e = rope_block(pq, kk, c3, sn3, t13, t23)
                if own:
                    R.wait("vector", s_mm, cntQ[t])
                    if t >= 2:
                        R.wait("vector", s_qr, 16 * (t - 1))
                    t13q = t1q[:, :].rearrange("p (h j) -> p h j", j=32)
                    t23q = t2q[:, :].rearrange("p (h j) -> p h j", j=32)
                    for ob in range(2):
                        pq = psum_b[:, rw(ob, 512)].rearrange("p (h x) -> p h x", x=64)
                        qq = q_tile[sl][:, rw(ob, 512)].rearrange("p (h x) -> p h x", x=64)
                        e = rope_block(pq, qq, c3, sn3, t13q, t23q)
                R.inc(e, s_rope)
                ropecnt[t] = s_rope.n

            def emit_vevac(t):
                sl = t % 2
                R.wait("scalar", s_mm, cntV[t])
                if t >= 2:
                    R.wait("scalar", s_vr, 16 * (t - 1))
                for ob in range(2):
                    sview = vstage[sl][:, :].rearrange("p (g x) -> p g x", x=65)[
                        :, ob * 8:(ob + 1) * 8, 0:64]
                    pv = psum_a[:, rw(2 + ob, 512)].rearrange("p (g x) -> p g x", x=64)
                    e = R.call("scalar", "activation", sview, pv, AF.Copy)
                R.inc(e, s_vs)

            def emit_stores(t):
                sl = t % 2
                own = t < NTOWN
                R.wait("sync", s_rope, ropecnt[t])
                R.dma("sync", KR_d[rw(t), :], kv_tile[sl][:, :], s_kr)
                if own:
                    R.dma("sync", QR_d[rw(t), :], q_tile[sl][:, :], s_qr)
                R.wait("sync", s_vs, t + 1)
                R.dma("sync", VR_d[t], vstage[sl][:, :], s_vr)

            for t in range(NT + GRP):
                if t < NT:
                    emit_loads_ln(t)
                    emit_qkv_mm(t)
                tt = t - GRP
                if tt >= 0:
                    emit_rope(tt)
                    emit_vevac(tt)
                    emit_stores(tt)
                    norms_job(tt, ropecnt[tt], tt < NTOWN)

            if "p25" not in skip:
                R.wait("tensor", s_nrm, s_nrm.n)
                R.wait("tensor", s_rope, s_rope.n)
                R.wait("tensor", s_vs, s_vs.n)
                R.wait("tensor", s_cst, 16)
                e = R.call("tensor", "transpose", psum_b[0:32, 0:128], kqm[:, :],
                           ident_sb[:, :])
                R.inc(e, s_bnd)
                R.wait("vector", s_bnd, 1)
                e = R.call("vector", "reduce_max", m2_sb[:, :], psum_b[0:32, 0:128],
                           mybir.AxisListType.X)
                R.inc(e, s_bnd)
                R.wait("tensor", s_bnd, 2)
                e = R.call("tensor", "transpose", psum_b[0:1, 0:32], m2_sb[:, :],
                           ident_sb[0:32, 0:32])
                R.inc(e, s_bnd)
                R.wait("vector", s_bnd, 3)
                R.call("vector", "tensor_copy", bb_row[:, :], psum_b[0:1, 0:32])
                R.drain("vector")
                R.call("vector", "tensor_tensor", bbp[:, :], bb_row[:, 0:NH],
                       bb_row[:, NH:32], ALU.mult)
                R.drain("vector")
                e = R.call("vector", "engine_nop")
                R.inc(e, s_bnd)
                R.wait("scalar", s_bnd, 4)
                e = R.call("scalar", "activation", bbp[:, :], bbp[:, :], AF.Sqrt)
                R.inc(e, s_bnd)
                R.wait("vector", s_bnd, 5)
                R.call("vector", "reduce_max", bb_row[:, 0:1], bbp[:, :],
                       mybir.AxisListType.X)
                R.drain("vector")
                e = R.call("vector", "tensor_scalar", bb_row[:, 0:1], bb_row[:, 0:1],
                           -0.125, None, ALU.mult)
                R.inc(e, s_bnd)
                R.wait("tensor", s_bnd, 6)
                e = R.call("tensor", "matmul", psum_b[0:128, 32:33], ones_sb[:, :],
                           bb_row[:, 0:1], start=True, stop=True)
                R.inc(e, s_bnd)
                R.wait("scalar", s_bnd, 7)
                e = R.call("scalar", "copy", negb[:, 0:1], psum_b[0:128, 32:33])
                R.inc(e, s_negb)
            R.replay()

        if phases < 3:
            return nc

        p2_tot = {s.name: s.n for s in sems.values()}

        with ExitStack() as SB:
            a3 = SB.enter_context
            sat = a3(nc.sbuf_tensor("sat", [128, FC * TOWN], BF16))
            with ExitStack() as SB1:
                a31 = SB1.enter_context
                kT = [a31(nc.sbuf_tensor(f"kT{i}", [128, L], BF16)) for i in range(2)]
                qT = [a31(nc.sbuf_tensor(f"qT{i}", [128, TOWN], BF16)) for i in range(2)]
                vp = [a31(nc.sbuf_tensor(f"vp{i}", [128, 64 * 65], BF16)) for i in range(4)]
                sp = a31(nc.sbuf_tensor("sp", [128, 2 * 1024], BF16))
                srec = [a31(nc.sbuf_tensor(f"srec{i}", [1, 512], BF16)) for i in range(2)]
                oun = [a31(nc.sbuf_tensor(f"oun{i}", [65, 512], F32)) for i in range(4)]
                sbc = [a31(nc.sbuf_tensor(f"sbc{i}", [64, 512], F32)) for i in range(2)]

                R = Rec(nc)
                Sv = [psum_a[:, 0:1024], psum_a[:, 1024:2048]]
                psO = [psum_b[0:65, 0:512], psum_b[0:65, 512:1024]]
                psBC = psum_b[0:64, 1024:1536]

                for eng in ("tensor", "scalar", "vector"):
                    R.wait(eng, s_rope, p2_tot["s_rope"])
                    R.wait(eng, s_vs, p2_tot["s_vs"])
                    R.wait(eng, s_negb, 1)
                R.wait("sync", s_kr, p2_tot["s_kr"])
                R.wait("sync", s_qr, p2_tot["s_qr"])
                R.wait("gpsimd", s_vr, p2_tot["s_vr"])

                glob_g = 0
                exp_at = {}
                pv_at = {}
                smm_at = {}
                last_smm_of_pair = [0] * 8
                last_pv_of_pair = [0] * 8
                oev_cnt_at = 0
                bcm_hist = []
                att_hist = []
                oun_free = {}
                pending_norm = []
                rec_hist = []

                def do_normalize(hp_, qb_):
                    for e_ in range(2):
                        slot = (qb_ % 2) * 2 + e_
                        R.wait("tensor", s_rec, rec_hist.pop(0))
                        R.wait("tensor", s_bce, s_bce.n)
                        e = R.call("tensor", "matmul", psBC, ones_bf[:, :],
                                   srec[e_][:, :], start=True, stop=True)
                        R.inc(e, s_bcm)
                        bcm_hist.append(s_bcm.n)
                        R.wait("vector", s_bcm, s_bcm.n)
                        if len(att_hist) >= 2:
                            R.wait("vector", s_att, att_hist[-2])
                        e = R.call("vector", "tensor_copy", sbc[e_][:, :], psBC)
                        R.inc(e, s_bce)
                        R.wait("vector", s_bce, s_bce.n)
                        e = R.call("vector", "tensor_tensor",
                                   sat[64 * e_:64 * e_ + 64,
                                       hp_ * TOWN + qb_ * 512:
                                       hp_ * TOWN + (qb_ + 1) * 512],
                                   oun[slot][0:64, :], sbc[e_][:, :], ALU.mult)
                        R.inc(e, s_att)
                        att_hist.append(s_att.n)
                        oun_free[slot] = s_att.n

                for hp in range(8):
                    if hp >= 2:
                        R.wait("sync", s_smm, last_smm_of_pair[hp - 2])
                    R.dma("sync", kT[hp % 2][:, :], KR_d[:, rw(hp)], s_kt,
                          transpose=True)
                    R.dma("sync", qT[hp % 2][:, :], QR_d[:, rw(hp)], s_qt,
                          transpose=True)
                    for e_ in range(2):
                        h = 2 * hp + e_
                        if hp >= 2:
                            R.wait("gpsimd", s_pv, last_pv_of_pair[hp - 2])
                        R.dma("gpsimd",
                              vp[(hp % 2) * 2 + e_][:, :].rearrange(
                                  "p (c f) -> p c f", f=65),
                              VR_d.rearrange("c p f -> p c f")[:, :, h * 65:(h + 1) * 65],
                              s_vp)
                    R.wait("tensor", s_kt, 16 * (hp + 1))
                    R.wait("tensor", s_qt, 16 * (hp + 1))
                    R.wait("tensor", s_vp, 32 * (hp + 1))
                    for qb in range(4):

                        def emit_pv(gg, c):
                            R.wait("tensor", s_exp, exp_at[gg])
                            if c == 0:
                                R.wait("tensor", s_oev, oev_cnt_at)
                            sl2 = gg % 2
                            for e_ in range(2):
                                e = R.call("tensor", "matmul", psO[e_],
                                           vp[(hp % 2) * 2 + e_][:, c * 65:(c + 1) * 65],
                                           sp[:, sl2 * 1024 + e_ * 512:
                                              sl2 * 1024 + (e_ + 1) * 512],
                                           start=(c == 0), stop=(c == 63))
                                if c == 63:
                                    R.inc(e, s_pvO)
                            if c == 63:
                                e2 = R.call("tensor", "nop")
                                R.inc(e2, s_pv)
                            else:
                                R.inc(e, s_pv)
                            pv_at[gg] = s_pv.n

                        for c in range(64):
                            gg = glob_g
                            sl2 = gg % 2
                            if gg >= 2:
                                R.wait("tensor", s_exp, exp_at[gg - 2])
                            for e_ in range(2):
                                e = R.call("tensor", "matmul",
                                           Sv[sl2][:, e_ * 512:(e_ + 1) * 512],
                                           kT[hp % 2][64 * e_:64 * e_ + 64, rw(c)],
                                           qT[hp % 2][64 * e_:64 * e_ + 64,
                                                      rw(qb, 512)],
                                           start=True, stop=True)
                            R.inc(e, s_smm)
                            smm_at[gg] = s_smm.n
                            R.wait("scalar", s_smm, smm_at[gg])
                            if gg >= 2:
                                R.wait("scalar", s_pv, pv_at[gg - 2])
                            e = R.call("scalar", "activation",
                                       sp[:, rw(sl2, 1024)], Sv[sl2][:, :],
                                       AF.Exp, bias=negb[:, 0:1], scale=0.125)
                            R.inc(e, s_exp)
                            exp_at[gg] = s_exp.n
                            if c >= 1:
                                emit_pv(gg - 1, c - 1)
                            if c == 16 and pending_norm:
                                do_normalize(*pending_norm.pop(0))
                            glob_g += 1
                        emit_pv(glob_g - 1, 63)

                        for e_ in range(2):
                            slot = (qb % 2) * 2 + e_
                            R.wait("vector", s_pvO, s_pvO.n - (1 - e_))
                            if slot in oun_free:
                                R.wait("vector", s_att, oun_free[slot])
                            e = R.call("vector", "tensor_copy", oun[slot][:, :],
                                       psO[e_])
                            R.inc(e, s_oev)
                        oev_cnt_at = s_oev.n
                        for e_ in range(2):
                            slot = (qb % 2) * 2 + e_
                            R.wait("vector", s_oev, oev_cnt_at - (1 - e_))
                            if len(bcm_hist) >= 2:
                                R.wait("vector", s_bcm, bcm_hist[-2 + e_])
                            e = R.call("vector", "reciprocal", srec[e_][:, :],
                                       oun[slot][64:65, :])
                            R.inc(e, s_rec)
                            rec_hist.append(s_rec.n)
                        pending_norm.append((hp, qb))
                    last_pv_of_pair[hp] = s_pv.n
                    last_smm_of_pair[hp] = s_smm.n
                while pending_norm:
                    do_normalize(*pending_norm.pop(0))
                R._allow_lp = True

                if debug:
                    R.wait("sync", s_att, s_att.n)
                    R.dma("sync", SATdbg_d, sat[:, :], s_kt)
                    R.wait("sync", s_kt, s_kt.n)
                R.replay()

            if phases < 4:
                return nc
            p3_tot = {s.name: s.n for s in sems.values()}

            with ExitStack() as SB2:
                a4 = SB2.enter_context
                wproj_sb = a4(nc.sbuf_tensor("wproj_sb", [128, FC * DM], BF16))
                bpr_sb = a4(nc.sbuf_tensor("bpr_sb", [1, DM], F32))
                bias_bc = a4(nc.sbuf_tensor("bias_bc", [128, DM], F32))
                y_f = [a4(nc.sbuf_tensor(f"y_f{i}", [128, DM], F32)) for i in range(2)]
                x_res = [a4(nc.sbuf_tensor(f"x_res{i}", [128, DM], F32)) for i in range(2)]
                x2b = [a4(nc.sbuf_tensor(f"x2b{i}", [128, DM], F32)) for i in range(2)]
                x3b = [a4(nc.sbuf_tensor(f"x3b{i}", [128, DM], BF16)) for i in range(2)]
                t3f = a4(nc.sbuf_tensor("t3f", [128, DM], F32))
                st4 = [a4(nc.sbuf_tensor(f"st4{i}", [128, 12], F32)) for i in range(2)]
                ag4 = [a4(nc.sbuf_tensor(f"ag4{i}", [128, 2], F32)) for i in range(2)]
                vr4 = [a4(nc.sbuf_tensor(f"vr4{i}", [128, 1], F32)) for i in range(2)]
                rs4 = [a4(nc.sbuf_tensor(f"rs4{i}", [128, 1], F32)) for i in range(2)]

                R = Rec(nc)
                R.dma("gpsimd", wproj_sb[:, :].rearrange("p (c o) -> p c o", o=DM),
                      wproj_d.rearrange("(c p) o -> p c o", p=128), s_wp)
                R.dma("gpsimd", bpr_sb[:, :], bproj_d[:, :], s_cst)
                R.wait("tensor", s_att, p3_tot["s_att"])
                R.wait("tensor", s_wp, 16)
                R.wait("tensor", s_cst, s_cst.n)
                for j in range(2):
                    e = R.call("tensor", "matmul", psum_b[0:128, rw(j, 512)],
                               ones_sb[:, :], bpr_sb[:, rw(j, 512)],
                               start=True, stop=True)
                    R.inc(e, s_bnd)
                R.wait("scalar", s_bnd, s_bnd.n)
                e = R.call("scalar", "copy", bias_bc[:, :], psum_b[0:128, 0:1024])
                R.inc(e, s_negb)
                R.wait("vector", s_negb, s_negb.n)

                def ln_chain(sl2, src_ap, is_bf_src=False):
                    R.call("vector", "bn_stats", st4[sl2][:, 0:6], src_ap[:, 0:512])
                    R.call("vector", "bn_stats", st4[sl2][:, 6:12], src_ap[:, 512:1024])
                    R.drain("vector")
                    R.call("vector", "bn_aggr", ag4[sl2][:, :], st4[sl2][:, :])
                    R.drain("vector")
                    e = R.call("vector", "tensor_scalar", vr4[sl2][:, :],
                               ag4[sl2][:, 1:2], 1e-5, None, ALU.add)
                    R.inc(e, s_lnv)
                    R.wait("scalar", s_lnv, s_lnv.n)
                    e = R.call("scalar", "activation", rs4[sl2][:, :], vr4[sl2][:, :],
                               AF.Sqrt)
                    R.inc(e, s_lns)
                    R.wait("vector", s_lns, s_lns.n)
                    R.call("vector", "reciprocal", rs4[sl2][:, :], rs4[sl2][:, :])
                    R.drain("vector")

                for t in range(NTOWN):
                    sl2 = t % 2
                    if t >= 2:
                        R.wait("gpsimd", s_x2w, t - 1)
                    R.dma("gpsimd", x_res[sl2][:, :], x_d[rw(t), :], s_xr)
                    if t >= 2:
                        R.wait("tensor", s_yev, t - 1)
                    for ob in range(2):
                        for i in range(FC):
                            e = R.call("tensor", "matmul",
                                       psum_a[:, (t % 2) * 1024 + ob * 512:
                                              (t % 2) * 1024 + (ob + 1) * 512],
                                       sat[:, i * TOWN + t * 128:i * TOWN + (t + 1) * 128],
                                       wproj_sb[:, i * DM + ob * 512:
                                               i * DM + (ob + 1) * 512],
                                       start=(i == 0), stop=(i == FC - 1))
                    R.inc(e, s_pmm)
                    R.wait("vector", s_pmm, s_pmm.n)
                    py = psum_a[:, (t % 2) * 1024:(t % 2) * 1024 + 1024]
                    e = R.call("vector", "tensor_tensor", y_f[sl2][:, :], py,
                               bias_bc[:, :], ALU.add)
                    R.inc(e, s_yev)
                    R.drain("vector")
                    ln_chain(sl2, y_f[sl2])
                    R.call("vector", "tensor_scalar", t3f[:, :], y_f[sl2][:, :],
                           ag4[sl2][:, 0:1], rs4[sl2][:, :], ALU.subtract, ALU.mult)
                    R.drain("vector")
                    R.wait("vector", s_xr, 16 * (t + 1))
                    if t >= 2:
                        R.wait("vector", s_x2s, 16 * (t - 1))
                    e = R.call("vector", "tensor_tensor", x2b[sl2][:, :], t3f[:, :],
                               x_res[sl2][:, :], ALU.add)
                    R.inc(e, s_x2w)
                    R.drain("vector")
                    ln_chain(sl2, x2b[sl2])
                    if t >= 2:
                        R.wait("vector", s_x3, 16 * (t - 1))
                    e = R.call("vector", "tensor_scalar", x3b[sl2][:, :], x2b[sl2][:, :],
                               ag4[sl2][:, 0:1], rs4[sl2][:, :], ALU.subtract, ALU.mult)
                    R.inc(e, s_x3v)
                    R.wait("sync", s_x2w, t + 1)
                    R.dma("sync", X2S_d[rw(t), :], x2b[sl2][:, :], s_x2s)
                    R.wait("sync", s_x3v, t + 1)
                    R.dma("sync", X3_d[rw(t), :], x3b[sl2][:, :], s_x3)

                R.replay()

        if phases < 5:
            return nc
        p4_tot = {s.name: s.n for s in sems.values()}

        with ExitStack() as SC:
            a5 = SC.enter_context
            x3th = a5(nc.sbuf_tensor("x3th", [128, FC * HALF], BF16))
            hbuf = a5(nc.sbuf_tensor("hbuf", [128, HC * HALF], BF16))
            gbuf = a5(nc.sbuf_tensor("gbuf", [128, HC * HALF], BF16))
            w1b = [a5(nc.sbuf_tensor(f"w1b{i}", [128, FC * 128], BF16)) for i in range(2)]
            w2b = [a5(nc.sbuf_tensor(f"w2b{i}", [128, FC * 128], BF16)) for i in range(2)]
            hwb = [a5(nc.sbuf_tensor(f"hwb{i}", [128, HC * 128], BF16)) for i in range(2)]
            tsil = [a5(nc.sbuf_tensor(f"tsil{i}", [128, 512], F32)) for i in range(2)]
            y3s = [a5(nc.sbuf_tensor(f"y3s{i}", [128, 512], BF16)) for i in range(2)]
            y3tile = [a5(nc.sbuf_tensor(f"y3tile{i}", [128, DM], BF16)) for i in range(2)]
            x2t = [a5(nc.sbuf_tensor(f"x2t{i}", [128, DM], F32)) for i in range(2)]
            x3o = [a5(nc.sbuf_tensor(f"x3o{i}", [128, DM], F32)) for i in range(2)]
            t5f = a5(nc.sbuf_tensor("t5f", [128, DM], F32))
            st5 = a5(nc.sbuf_tensor("st5", [128, 12], F32))
            ag5 = a5(nc.sbuf_tensor("ag5", [128, 2], F32))
            vr5 = a5(nc.sbuf_tensor("vr5", [128, 1], F32))
            rs5 = a5(nc.sbuf_tensor("rs5", [128, 1], F32))

            R = Rec(nc)
            for eng in ("tensor", "scalar", "vector", "gpsimd"):
                R.wait(eng, s_pmm, p4_tot["s_pmm"])
                R.wait(eng, s_x2s, p4_tot["s_x2s"])
                R.wait(eng, s_yev, p4_tot["s_yev"])
                R.wait(eng, s_x3, p4_tot["s_x3"])
            R.wait("sync", s_x3, p4_tot["s_x3"])

            def emit_stageE_tile(t):
                sl = t % 2
                R.wait("sync", s_y3d, 16 * 16 * (t // 8 + 1))
                if t >= 2:
                    R.wait("sync", s_x3o, t - 1)
                R.dma("sync", y3tile[sl][:, :], Y3_d[:, rw(t)], s_y3t, transpose=True)
                if t >= 2:
                    R.wait("gpsimd", s_x3o, t - 1)
                R.dma("gpsimd", x2t[sl][:, :], X2S_d[rw(t), :], s_x2l)
                R.wait("vector", s_y3t, 16 * (t + 1))
                R.wait("vector", s_x2l, 16 * (t + 1))
                R.call("vector", "bn_stats", st5[:, 0:6], y3tile[sl][:, 0:512])
                R.call("vector", "bn_stats", st5[:, 6:12], y3tile[sl][:, 512:1024])
                R.drain("vector")
                R.call("vector", "bn_aggr", ag5[:, :], st5[:, :])
                R.drain("vector")
                e = R.call("vector", "tensor_scalar", vr5[:, :], ag5[:, 1:2],
                           1e-5, None, ALU.add)
                R.inc(e, s_lnv)
                R.wait("scalar", s_lnv, s_lnv.n)
                e = R.call("scalar", "activation", rs5[:, :], vr5[:, :], AF.Sqrt)
                R.inc(e, s_lns)
                R.wait("vector", s_lns, s_lns.n)
                R.call("vector", "reciprocal", rs5[:, :], rs5[:, :])
                R.drain("vector")
                R.call("vector", "tensor_scalar", t5f[:, :], y3tile[sl][:, :],
                       ag5[:, 0:1], rs5[:, :], ALU.subtract, ALU.mult)
                R.drain("vector")
                if t >= 2:
                    R.wait("vector", s_out, 16 * (t - 1))
                e = R.call("vector", "tensor_tensor", x3o[t % 2][:, :], t5f[:, :],
                           x2t[sl][:, :], ALU.add)
                R.inc(e, s_x3o)
                R.wait("sync", s_x3o, t + 1)
                R.dma("sync", out_d[rw(t), :], x3o[t % 2][:, :], s_out)

            pending_E = []

            fmm_at = {}
            silv = {}
            hv_at = {}
            w12_use = {}
            hw_use = {}
            y3s_last = {}
            for hf in range(2):
                tok0 = hf * HALF
                if hf == 1:
                    R.wait("tensor", s_y3e, s_y3e.n)
                    R.wait("vector", s_dmm, s_dmm.n)
                    R.wait("scalar", s_y3e, s_y3e.n)
                    R.wait("sync", s_fmm, s_fmm.n)
                for i in range(FC):
                    R.dma("sync", x3th[:, rw(i, HALF)],
                          X3_d[rw(hf, HALF), rw(i)], s_x3t, transpose=True)
                R.wait("tensor", s_x3t, s_x3t.n)
                for o in range(HC):
                    par = o % 2
                    if par in w12_use:
                        R.wait("gpsimd", s_fmm, w12_use[par])
                    R.dma("gpsimd", w1b[par][:, :].rearrange("p (c o2) -> p c o2", o2=128),
                          w1t_d.rearrange("(c p) o2 -> p c o2", p=128)[:, :, rw(o)], s_w12)
                    R.dma("gpsimd", w2b[par][:, :].rearrange("p (c o2) -> p c o2", o2=128),
                          w2t_d.rearrange("(c p) o2 -> p c o2", p=128)[:, :, rw(o)], s_w12)
                    R.wait("tensor", s_w12, s_w12.n)
                    if o % 4 == 3 and pending_E:
                        emit_stageE_tile(pending_E.pop(0))
                    pX = psum_a if par == 0 else psum_b
                    for tb in range(2):
                        key = (hf, o - 2, tb)
                        if o >= 2:
                            R.wait("tensor", s_sil, silv[key])
                            R.wait("tensor", s_hv, hv_at[key])
                        for which, (wb, pcol) in enumerate(
                                ((w1b[par], tb * 1024), (w2b[par], tb * 1024 + 512))):
                            for i in range(FC):
                                e = R.call("tensor", "matmul",
                                           pX[:, pcol:pcol + 512],
                                           wb[:, rw(i)],
                                           x3th[:, i * HALF + tb * 512:
                                                i * HALF + (tb + 1) * 512],
                                           start=(i == 0), stop=(i == FC - 1))
                            R.inc(e, s_fmm)
                            fmm_at[(hf, o, tb, "w1" if which == 0 else "w2")] = s_fmm.n
                        R.wait("scalar", s_fmm, fmm_at[(hf, o, tb, "w1")])
                        pk = (hf, o - 1, tb) if o >= 1 else (hf - 1, HC - 1, tb)
                        if pk in hv_at:
                            R.wait("scalar", s_hv, hv_at[pk])
                        e = R.call("scalar", "activation", tsil[tb][:, :],
                                   pX[:, tb * 1024:tb * 1024 + 512], AF.Silu,
                                   scale=s1_sb[:, o:o + 1])
                        R.inc(e, s_sil)
                        silv[(hf, o, tb)] = s_sil.n
                        R.wait("vector", s_sil, silv[(hf, o, tb)])
                        R.wait("vector", s_fmm, fmm_at[(hf, o, tb, "w2")])
                        e = R.call("vector", "scalar_tensor_tensor",
                                   hbuf[:, o * HALF + tb * 512:o * HALF + (tb + 1) * 512],
                                   pX[:, tb * 1024 + 512:tb * 1024 + 1024],
                                   s2_sb[:, o:o + 1], tsil[tb][:, :],
                                   ALU.mult, ALU.mult)
                        R.inc(e, s_hv)
                        hv_at[(hf, o, tb)] = s_hv.n
                    w12_use[par] = s_fmm.n
                hv_stageA = s_hv.n

                gelu_at = {}
                for o2 in range(HC):
                    par = o2 % 2
                    if par in hw_use:
                        R.wait("gpsimd", *hw_use[par])
                    R.dma("gpsimd", hwb[par][:, :].rearrange("p (c o3) -> p c o3", o3=128),
                          h1t_d.rearrange("(c p) o3 -> p c o3", p=128)[:, :, rw(o2)], s_hwb)
                    R.wait("tensor", s_hwb, s_hwb.n)
                    R.wait("tensor", s_hv, hv_stageA)
                    pX = psum_a if par == 0 else psum_b
                    for tb in range(2):
                        if o2 >= 2:
                            R.wait("tensor", s_gelu, gelu_at[("B", o2 - 2, tb)])
                        for i in range(HC):
                            e = R.call("tensor", "matmul", pX[:, rw(tb, 512)],
                                       hwb[par][:, rw(i)],
                                       hbuf[:, i * HALF + tb * 512:
                                            i * HALF + (tb + 1) * 512],
                                       start=(i == 0), stop=(i == HC - 1))
                        R.inc(e, s_gmm)
                        R.wait("scalar", s_gmm, s_gmm.n)
                        e = R.call("scalar", "activation",
                                   gbuf[:, o2 * HALF + tb * 512:o2 * HALF + (tb + 1) * 512],
                                   pX[:, rw(tb, 512)], AF.Gelu, scale=sh1_sb[:, o2:o2 + 1])
                        R.inc(e, s_gelu)
                        gelu_at[("B", o2, tb)] = s_gelu.n
                    hw_use[par] = (s_gmm, s_gmm.n)
                gelu_stageB = s_gelu.n

                for o2 in range(HC):
                    par = o2 % 2
                    if par in hw_use:
                        R.wait("gpsimd", *hw_use[par])
                    R.dma("gpsimd", hwb[par][:, :].rearrange("p (c o3) -> p c o3", o3=128),
                          h2t_d.rearrange("(c p) o3 -> p c o3", p=128)[:, :, rw(o2)], s_hwb)
                    R.wait("tensor", s_hwb, s_hwb.n)
                    R.wait("tensor", s_gelu, gelu_stageB)
                    pX = psum_a if par == 0 else psum_b
                    for tb in range(2):
                        if o2 >= 2:
                            R.wait("tensor", s_gelu, gelu_at[("C", o2 - 2, tb)])
                        for i in range(HC):
                            e = R.call("tensor", "matmul", pX[:, rw(tb, 512)],
                                       hwb[par][:, rw(i)],
                                       gbuf[:, i * HALF + tb * 512:
                                            i * HALF + (tb + 1) * 512],
                                       start=(i == 0), stop=(i == HC - 1))
                        R.inc(e, s_gmm)
                        R.wait("scalar", s_gmm, s_gmm.n)
                        e = R.call("scalar", "activation",
                                   hbuf[:, o2 * HALF + tb * 512:o2 * HALF + (tb + 1) * 512],
                                   pX[:, rw(tb, 512)], AF.Gelu, scale=sh2_sb[:, o2:o2 + 1])
                        R.inc(e, s_gelu)
                        gelu_at[("C", o2, tb)] = s_gelu.n
                    hw_use[par] = (s_gmm, s_gmm.n)
                gelu_stageC = s_gelu.n

                y3e_at = {}
                y3d_at = {}
                for o3 in range(FC):
                    par = o3 % 2
                    if par in hw_use:
                        R.wait("gpsimd", *hw_use[par])
                    R.dma("gpsimd", hwb[par][:, :].rearrange("p (c o4) -> p c o4", o4=128),
                          w3t_d.rearrange("(c p) o4 -> p c o4", p=128)[:, :, rw(o3)], s_hwb)
                    R.wait("tensor", s_hwb, s_hwb.n)
                    R.wait("tensor", s_gelu, gelu_stageC)
                    pX = psum_a if par == 0 else psum_b
                    for tb in range(2):
                        if o3 >= 2:
                            R.wait("tensor", s_y3e, y3e_at[(o3 - 2, tb)])
                        for i in range(HC):
                            e = R.call("tensor", "matmul", pX[:, rw(tb, 512)],
                                       hwb[par][:, rw(i)],
                                       hbuf[:, i * HALF + tb * 512:
                                            i * HALF + (tb + 1) * 512],
                                       start=(i == 0), stop=(i == HC - 1))
                        R.inc(e, s_dmm)
                        R.wait("vector", s_dmm, s_dmm.n)
                        if tb in y3s_last:
                            R.wait("vector", s_y3d, y3s_last[tb])
                        e = R.call("vector", "tensor_scalar", y3s[tb][:, :],
                                   pX[:, rw(tb, 512)], s3_sb[:, o3:o3 + 1], None,
                                   ALU.mult)
                        R.inc(e, s_y3e)
                        y3e_at[(o3, tb)] = s_y3e.n
                        R.wait("sync", s_y3e, s_y3e.n)
                        R.dma("sync", Y3_d[rw(o3), tok0 + tb * 512:tok0 + (tb + 1) * 512],
                              y3s[tb][:, :], s_y3d)
                        y3d_at[(o3, tb)] = s_y3d.n
                        y3s_last[tb] = s_y3d.n
                    hw_use[par] = (s_dmm, s_dmm.n)
                pending_E.extend(range(hf * 8, hf * 8 + 8))

            while pending_E:
                emit_stageE_tile(pending_E.pop(0))

            R.wait("sync", s_out, 16 * NTOWN)
            R.replay()

    return nc



def _prep_inputs(inputs):
    x = np.asarray(inputs["L0_attn_x"], dtype=np.float32)
    wqkv = np.asarray(inputs["L0_attn_w_qkv"], dtype=np.float32)
    wproj = np.asarray(inputs["L0_attn_w_proj"], dtype=np.float32)
    bproj = np.asarray(inputs["L0_attn_b_proj"], dtype=np.float32)
    w1 = np.asarray(inputs["L0_ffn_w1_w"], dtype=np.float32)
    w2 = np.asarray(inputs["L0_ffn_w2_w"], dtype=np.float32)
    h1 = np.asarray(inputs["L0_ffn_h1_w"], dtype=np.float32)
    h2 = np.asarray(inputs["L0_ffn_h2_w"], dtype=np.float32)
    w3 = np.asarray(inputs["L0_ffn_w3_w"], dtype=np.float32)

    idx = []
    for blk in range(2):
        for h in range(NH):
            base = blk * DM + h * HD
            idx += [base + 2 * j for j in range(32)]
            idx += [base + 2 * j + 1 for j in range(32)]
    idx += list(range(2 * DM, 3 * DM))
    wqkv_p = np.ascontiguousarray(wqkv[:, idx])

    a = np.arange(0, HD, 2, dtype=np.float32) / np.float32(HD)
    theta = (np.float32(1.0) / (np.float32(10000.0) ** a)) * np.float32(3.0)
    ang = np.arange(L, dtype=np.float32)[:, None] * theta[None, :]
    cos = np.cos(ang).astype(np.float32)
    sin = np.sin(ang).astype(np.float32)
    cos_rep = np.tile(cos, (1, 8))
    sin_rep = np.tile(sin, (1, 8))

    shared = {
        "wqkv": wqkv_p,
        "wproj": np.ascontiguousarray(wproj),
        "bproj": bproj.reshape(1, DM),
        "w1t": np.ascontiguousarray(w1.T),
        "w2t": np.ascontiguousarray(w2.T),
        "h1t": np.ascontiguousarray(h1.T),
        "h2t": np.ascontiguousarray(h2.T),
        "w3t": np.ascontiguousarray(w3.T),
        "scl": np.ascontiguousarray(np.concatenate(
            [np.asarray(inputs["L0_ffn_w1_s"], np.float32).reshape(HC, 128).T,
             np.asarray(inputs["L0_ffn_w2_s"], np.float32).reshape(HC, 128).T,
             np.asarray(inputs["L0_ffn_h1_s"], np.float32).reshape(HC, 128).T,
             np.asarray(inputs["L0_ffn_h2_s"], np.float32).reshape(HC, 128).T,
             np.asarray(inputs["L0_ffn_w3_s"], np.float32).reshape(FC, 128).T],
            axis=1)),
        "ident": np.eye(128, dtype=np.float32),
    }
    in_maps = []
    for c in range(NCORES):
        b, s = c // 4, c % 4
        m = dict(shared)
        m["x"] = np.ascontiguousarray(np.roll(x[b], -s * TOWN, axis=0))
        m["tabc"] = np.ascontiguousarray(np.roll(cos_rep, -s * TOWN, axis=0))
        m["tabs"] = np.ascontiguousarray(np.roll(sin_rep, -s * TOWN, axis=0))
        in_maps.append(m)
    return in_maps


def kernel(**inputs):
    in_maps = _prep_inputs(inputs)
    nc = build_nc()
    res = run_bass_kernel_spmd(nc, in_maps, core_ids=list(range(NCORES)))
    out = np.concatenate([np.asarray(r["out"], dtype=np.float32)
                          for r in res.results], axis=0)
    return out.reshape(2, L, DM)
